# version 26
# baseline (speedup 1.0000x reference)
import contextlib
import numpy as np

import concourse.bass as bass
import concourse.mybir as mybir
from concourse.alu_op_type import AluOpType as OP
from concourse.bass_utils import run_bass_kernel_spmd

F32 = mybir.dt.float32
AF = mybir.ActivationFunctionType
AX = mybir.AxisListType

DM = 1024
DI = 2048
NS = 16
HID = 4096
KC = 4
T = 256
L_FULL = 4096
NCHIP = 8
EPS = 1e-5

NC_DM = DM // 128
NC_DI = DI // 128
NC_HID = HID // 128

ENGINES = ("sync", "tensor", "vector", "scalar", "gpsimd")
SAME_ENGINE_SYNC = True
BENCH = 0


class _Op:
    __slots__ = ("eng", "fn", "dma", "stream", "deps", "drain", "token", "tag", "idx")

    def __init__(self, eng, fn, dma, tag, stream=None):
        self.eng, self.fn, self.dma, self.tag = eng, fn, dma, tag
        self.stream = stream
        self.deps = []
        self.drain = False
        self.token = None
        self.idx = 0


class Plan:

    def __init__(self):
        self.q = {e: [] for e in ENGINES}
        self.last_w = {}
        self.readers = {}
        self.all_ops = []
        self.stream_last = {}
        self.stream_cnt = {}

    def add(self, eng, fn, reads=(), writes=(), dma=False, tag="", stream=None):
        op = _Op(eng, fn, dma, tag, stream)
        deps = []

        def dep_on(d):
            if d is None or d is op:
                return
            if d.dma or d.eng != eng:
                if d not in deps:
                    deps.append(d)
            elif SAME_ENGINE_SYNC and eng != "tensor":
                if d not in deps:
                    deps.append(d)

        for r in reads:
            dep_on(self.last_w.get(r))
        for w in writes:
            dep_on(self.last_w.get(w))
            for rd in self.readers.get(w, ()):
                dep_on(rd)
        op.deps = deps
        for r in reads:
            self.readers.setdefault(r, []).append(op)
        for w in writes:
            self.last_w[w] = op
            self.readers[w] = []
        if dma:
            prev = self.stream_last.get(stream)
            if prev is not None and prev not in deps:
                deps.append(prev)
            self.stream_last[stream] = op
            self.stream_cnt[stream] = self.stream_cnt.get(stream, 0) + 16
            op.token = ("d_" + stream, self.stream_cnt[stream])
        op.idx = len(self.all_ops)
        self.all_ops.append(op)
        self.q[eng].append(op)
        return op

    def dma(self, eng, out, in_, stream, reads=(), writes=(), tag=""):
        def fn(E, out=out, in_=in_):
            return E.dma_start(out=out, in_=in_)
        return self.add(eng, fn, reads=reads, writes=writes, dma=True, tag=tag,
                        stream=stream)

    def finalize(self):
        eng_cnt = {e: 0 for e in ENGINES}
        needed = set()
        for o in self.all_ops:
            for d in o.deps:
                if not d.dma:
                    needed.add(id(d))
        for e in ENGINES:
            for o in self.q[e]:
                if not o.dma and id(o) in needed:
                    eng_cnt[e] += 1
                    o.token = ("s_" + e, eng_cnt[e])
        self.final_counts = {"d_" + k: v for k, v in self.stream_cnt.items()}
        for e in ENGINES:
            self.final_counts["s_" + e] = eng_cnt[e]
        for o in self.all_ops:
            for d in o.deps:
                assert d.idx < o.idx, f"dep cycle: {d.tag} -> {o.tag}"

    def emit(self, block, sems):
        watermark = {e: {} for e in ENGINES}

        def run(E, eng):
            for o in self.q[eng]:
                want = {}
                for d in o.deps:
                    s, v = d.token
                    if want.get(s, 0) < v:
                        want[s] = v
                for s, v in sorted(want.items()):
                    if watermark[eng].get(s, 0) >= v:
                        continue
                    E.wait_ge(sems[s], v)
                    watermark[eng][s] = v
                ins = o.fn(E)
                if o.token is not None:
                    s, _ = o.token
                    ins.then_inc(sems[s], 16 if o.dma else 1)

        block.sync(lambda E: run(E, "sync"))
        block.tensor(lambda E: run(E, "tensor"))
        block.vector(lambda E: run(E, "vector"))
        block.scalar(lambda E: run(E, "scalar"))
        block.gpsimd(lambda E: run(E, "gpsimd"))


def ceil_div(a, b):
    return -(-a // b)


_LAST_PLAN = [None]


def build_program(NT=16):
    L = NT * T
    nc = bass.Bass("TRN2", target_bir_lowering=False)
    p = Plan()

    d_in = {}
    def din(name, shape):
        d_in[name] = nc.dram_tensor(name, shape, F32, kind="ExternalInput").ap()
        return d_in[name]

    x_t = din("x_t", [DM, L])
    wup1_t = din("wup1_t", [DM, DI])
    wup2_t = din("wup2_t", [DM, DI])
    wll_t = din("wll_t", [DI, 2080])
    wdown_t = din("wdown_t", [DI, DM])
    wfc_t = din("wfc_t", [DM, HID])
    wfc_n = din("wfc_n", [HID, DM])
    wproj_t = din("wproj_t", [HID, DM])
    wproj_n = din("wproj_n", [DM, HID])
    u_t = din("u_t", [NS, HID])
    v_t = din("v_t", [DM, NS])
    fastb_t = din("fastb_t", [NS, NS])
    wmod = din("wmod", [DM, 6 * DM])
    din("cfm", [128, NC_DM])
    din("lng", [128, NC_DM])
    din("lnb", [128, NC_DM])
    din("bll", [128, 18])
    din("bdown", [128, NC_DM])
    din("convw", [128, NC_DI * KC])
    din("convb", [128, NC_DI])
    din("alog_fm", [128, NC_DI * NS])
    gate_in = din("gate_in", [1, 1])
    din("ident_in", [128, 128])

    out_t = nc.dram_tensor("out_t", [DM, L], F32, kind="ExternalOutput").ap()

    ternfcT = nc.dram_tensor("ternfcT", [DM, HID], F32, kind="Internal").ap()
    ternprojT = nc.dram_tensor("ternprojT", [HID, DM], F32, kind="Internal").ap()
    rsfc_d = nc.dram_tensor("rsfc_d", [1, HID], F32, kind="Internal").ap()
    us_d = nc.dram_tensor("us_d", [NS, HID], F32, kind="Internal").ap()
    rsproj_d = nc.dram_tensor("rsproj_d", [1, DM], F32, kind="Internal").ap()

    S = {}
    def sbt(name, shape, dtype=F32):
        S[name] = nc.alloc_sbuf_tensor(name, shape, dtype)
        return S[name]

    sbt("ident", [128, 128])
    for nm, w in [("lng_c", NC_DM), ("lnb_c", NC_DM), ("bdown_c", NC_DM),
                  ("convb_c", NC_DI), ("silu_c", NC_DM), ("cfm_c", NC_DM)]:
        sbt(nm, [128, w])
    sbt("bll_c", [128, 18])
    sbt("convw_c", [128, NC_DI * KC])
    sbt("A_fm", [128, NC_DI * NS])
    sbt("iA_fm", [128, NC_DI * NS])
    sbt("mod_c", [128, 48])
    for nm in ("f1_c", "sa_c", "ga_c", "f2_c", "sb_c", "gb_c"):
        sbt(nm, [128, NC_DM])
    sbt("scale_fc", [128, NC_HID]); sbt("scale_pj", [128, NC_DM])
    sbt("rs_fc", [128, NC_HID]); sbt("rs_pj", [128, NC_DM])
    sbt("v_c", [128, NC_DM * NS])
    sbt("fbgT", [NS, NS])
    sbt("gate_c", [NS, 1])
    sbt("ones_col", [128, 1])
    sbt("kconst", [128, 2])

    sbt("wbA", [128, 1024]); sbt("wbB", [128, 1024])
    sbt("wbC", [128, 1024]); sbt("wbD", [128, 1024])
    sbt("ubuf", [NS, 4 * 128])

    sbt("x0", [128, NC_DM * T]); sbt("x1", [128, NC_DM * T])
    sbt("h", [128, NC_DM * T])
    sbt("xn", [128, NC_DM * T])
    sbt("xmainA", [128, NC_DI * T]); sbt("xmainB", [128, NC_DI * T])
    sbt("xgateA", [128, NC_DI * T]); sbt("xgateB", [128, NC_DI * T])
    for _dn in ("deltaA", "deltaB", "deltaC", "deltaD"):
        sbt(_dn, [128, T])
    sbt("Bm", [NS, T]); sbt("Cm", [NS, T])
    sbt("Bm_rep", [128, NS * T])
    sbt("C_rep", [128, NS * T])
    sbt("AbarA", [128, (NS // 2) * T]); sbt("AbarB", [128, (NS // 2) * T])
    sbt("Gbuf", [128, (NS // 2) * T])
    sbt("K_state", [128, NC_DI * NS])
    sbt("convhalo", [128, NC_DI * (KC - 1)])
    sbt("Xmp", [128, T + KC - 1])
    sbt("yK", [128, T]); sbt("t1b", [128, T])
    sbt("sq0", [128, T]); sbt("sq1", [128, T])
    sbt("tmA", [128, T]); sbt("tmB", [128, T])
    sbt("scA", [128, T]); sbt("scB", [128, T])
    sbt("statr", [1, 3 * T])
    sbt("rstd_rep", [128, T]); sbt("q_rep", [128, T])
    for _fn in ("fA0", "fA1", "fB0", "fB1"):
        sbt(_fn, [128, T])

    PS = [nc.alloc_psum_tensor(f"pb{i}", [128, 512], F32) for i in range(8)]

    def pslot(b, c):
        return PS[b][:, c * T:(c + 1) * T]

    bank_open = [False] * 8

    def MM(bank, col, lhsT, rhs, *, last_in_bank, reads, tag="mm", mslice=None,
           ncols=T):
        st = not bank_open[bank]
        bank_open[bank] = not last_in_bank
        m = 128 if mslice is None else mslice
        outap = PS[bank][0:m, col * T:col * T + ncols]

        def fn(E, outap=outap, lhsT=lhsT, rhs=rhs, st=st, sp=last_in_bank):
            return E.matmul(outap, lhsT, rhs, start=st, stop=sp)
        return p.add("tensor", fn, reads=reads, writes=(f"pb{bank}",), tag=tag)

    def ACT(outap, inap, func, *, bias=0.0, scale=1.0, reads, writes, tag="act"):
        def fn(E, outap=outap, inap=inap, func=func, bias=bias, scale=scale):
            return E.activation(outap, inap, func, bias=bias, scale=scale)
        return p.add("scalar", fn, reads=reads, writes=writes, tag=tag)

    def TT(eng, outap, a, b, op, *, reads, writes, tag="tt"):
        def fn(E, outap=outap, a=a, b=b, op=op):
            return E.tensor_tensor(outap, a, b, op)
        return p.add(eng, fn, reads=reads, writes=writes, tag=tag)

    def TS(eng, outap, a, s1, op0, s2=None, op1=OP.bypass, *, reads, writes, tag="ts"):
        def fn(E, outap=outap, a=a, s1=s1, s2=s2, op0=op0, op1=op1):
            return E.tensor_scalar(outap, a, s1, s2, op0, op1)
        return p.add(eng, fn, reads=reads, writes=writes, tag=tag)

    def STT(eng, outap, in0, scal, in1, op0, op1, *, reads, writes, tag="stt"):
        def fn(E, outap=outap, in0=in0, scal=scal, in1=in1, op0=op0, op1=op1):
            return E.scalar_tensor_tensor(outap, in0, scal, in1, op0, op1)
        return p.add(eng, fn, reads=reads, writes=writes, tag=tag)

    def CP(eng, outap, inap, *, reads, writes, tag="cp"):
        def fn(E, outap=outap, inap=inap):
            return E.tensor_copy(outap, inap)
        return p.add(eng, fn, reads=reads, writes=writes, tag=tag)

    def RED(outap, inap, *, reads, writes, tag="red"):
        def fn(E, outap=outap, inap=inap):
            return E.tensor_reduce(outap, inap, AX.X, OP.add)
        return p.add("vector", fn, reads=reads, writes=writes, tag=tag)

    def RECIP(outap, inap, *, reads, writes, tag="recip"):
        def fn(E, outap=outap, inap=inap):
            return E.reciprocal(outap, inap)
        return p.add("vector", fn, reads=reads, writes=writes, tag=tag)

    def MEMSET(ap_, val, *, writes, tag="memset"):
        def fn(E, ap_=ap_, val=val):
            return E.memset(ap_, val)
        return p.add("vector", fn, writes=writes, tag=tag)

    def bcast_dma(outap, src_row_ap, stream, *, reads, writes, tag="bc"):
        ap = src_row_ap.unsqueeze(1).broadcast_to([1, 128, src_row_ap.shape[-1]])
        return p.dma("sync", outap, ap, stream, reads=reads, writes=writes, tag=tag)

    ALL_AB = tuple(f"Ab{n}" for n in range(NS))

    MEMSET(S["ones_col"][:], 1.0, writes=("ones_col",))
    MEMSET(S["kconst"][:, 0:1], -1.0, writes=("kconst",))
    MEMSET(S["kconst"][:, 1:2], 1.0, writes=("kconst",))
    MEMSET(S["K_state"][:], 0.0, writes=tuple(f"Kst{c}" for c in range(NC_DI)))
    MEMSET(S["convhalo"][:], 0.0, writes=tuple(f"convhalo{c}" for c in range(NC_DI)))

    if BENCH:
        def fn_b(E, nn=None):
            return E.matmul(PS[0][:, 0:nn], S["ident"][:], S["Bm_rep"][:, 0:nn],
                            start=True, stop=True)
        for rr in range(200):
            p.add("tensor", (lambda E, nn=512: fn_b(E, nn)), reads=("ident",),
                  writes=("pb0",), tag=f"bench512_{rr}")
        for rr in range(200):
            p.add("tensor", (lambda E, nn=256: fn_b(E, nn)), reads=("ident",),
                  writes=("pb0",), tag=f"bench256_{rr}")
        for rr in range(200):
            p.add("tensor", (lambda E, nn=384: fn_b(E, nn)), reads=("ident",),
                  writes=("pb0",), tag=f"bench384_{rr}")

    for nm, src in [("ident", "ident_in"), ("lng_c", "lng"), ("lnb_c", "lnb"),
                    ("bdown_c", "bdown"), ("convb_c", "convb"), ("bll_c", "bll"),
                    ("convw_c", "convw"), ("cfm_c", "cfm")]:
        p.dma("sync", S[nm][:], d_in[src], "qC", writes=(nm,), tag=f"ld_{nm}")
    p.dma("sync", S["A_fm"][:], d_in["alog_fm"], "qC", writes=("A_fm",), tag="ld_afm")
    p.dma("sync", S["fbgT"][:], fastb_t, "qC", writes=("fbgT",), tag="ld_fbt")
    for j in range(NC_DM):
        p.dma("sync", S["v_c"][:, j * NS:(j + 1) * NS], v_t[j * 128:(j + 1) * 128, :],
              "qC", writes=(f"v_c{j}",), tag="ld_v")
    p.dma("sync", S["gate_c"][:],
          gate_in[0:1, 0:1].unsqueeze(1).broadcast_to([1, NS, 1]),
          "qC", writes=("gate_c",), tag="ld_gate")

    ACT(S["iA_fm"][:], S["A_fm"][:], AF.Exp, reads=("A_fm",), writes=("iA_fm",), tag="a_exp")
    TS("vector", S["A_fm"][:], S["iA_fm"][:], -1.0, OP.mult,
       reads=("iA_fm",), writes=("A_fm",), tag="a_neg")
    RECIP(S["iA_fm"][:], S["A_fm"][:], reads=("A_fm",), writes=("iA_fm",), tag="a_recip")
    ACT(S["silu_c"][:], S["cfm_c"][:], AF.Silu, reads=("cfm_c",), writes=("silu_c",), tag="siluc")

    for g in range(6):
        for dj in range(NC_DM):
            wb = "wbA" if (g * NC_DM + dj) % 2 == 0 else "wbB"
            p.dma("sync", S[wb][:],
                  wmod[dj * 128:(dj + 1) * 128, g * 1024:(g + 1) * 1024],
                  f"{wb}0", writes=(f"{wb}0", f"{wb}1"), tag="ld_wmod")
            for j in range(8):
                MM(j, 0, S[wb][:, j * 128:(j + 1) * 128], S["silu_c"][:, dj:dj + 1],
                   last_in_bank=(dj == NC_DM - 1), reads=(f"{wb}{j // 4}", "silu_c"),
                   ncols=1, tag=f"mm_mod{g}_{dj}_{j}")
        for j in range(8):
            ec = g * 8 + j
            ACT(S["mod_c"][:, ec:ec + 1], PS[j][:, 0:1], AF.Copy,
                reads=(f"pb{j}",), writes=("mod_c",), tag=f"ev_mod{ec}")

    mc = S["mod_c"]
    CP("vector", S["sa_c"][:], mc[:, 0:8], reads=("mod_c",), writes=("sa_c",), tag="sa")
    TS("vector", S["f1_c"][:], mc[:, 8:16], 1.0, OP.add, reads=("mod_c",), writes=("f1_c",), tag="f1")
    CP("vector", S["sb_c"][:], mc[:, 24:32], reads=("mod_c",), writes=("sb_c",), tag="sbv")
    TS("vector", S["f2_c"][:], mc[:, 32:40], 1.0, OP.add, reads=("mod_c",), writes=("f2_c",), tag="f2")
    for src_off, dst in [(16, "ga_c"), (40, "gb_c")]:
        TS("vector", S["tmA"][:, 0:8], mc[:, src_off:src_off + 8], 0.0, OP.is_ge, 2.0, OP.mult,
           reads=("mod_c",), writes=("tmA",), tag="g_s0")
        TS("vector", S["tmA"][:, 0:8], S["tmA"][:, 0:8], -1.0, OP.add,
           reads=("tmA",), writes=("tmA",), tag="g_s1")
        ACT(S["tmB"][:, 0:8], mc[:, src_off:src_off + 8], AF.Abs,
            reads=("mod_c",), writes=("tmB",), tag="g_abs")
        TS("vector", S["tmB"][:, 0:8], S["tmB"][:, 0:8], 0.25, OP.max,
           reads=("tmB",), writes=("tmB",), tag="g_max")
        TT("vector", S[dst][:], S["tmA"][:, 0:8], S["tmB"][:, 0:8], OP.mult,
           reads=("tmA", "tmB"), writes=(dst,), tag="g_mul")

    def fn_fbg(E):
        return E.tensor_scalar(S["fbgT"][:], S["fbgT"][:], S["gate_c"][:, 0:1],
                               None, OP.mult, OP.bypass)
    p.add("vector", fn_fbg, reads=("fbgT", "gate_c"), writes=("fbgT",), tag="fbg")

    def ternarize(nat, tr, EE, DD, scale_nm, rs_nm, rs_flat, ternT, rsflat_nm, ternT_nm, tag):
        nch = EE // 128
        npieces = DD // 1024
        for r in range(nch):
            nb = "wbA" if r % 2 == 0 else "wbB"
            ab = "wbC" if r % 2 == 0 else "wbD"
            for q in range(npieces):
                p.dma("sync", S[nb][:, 0:1024],
                      nat[r * 128:(r + 1) * 128, q * 1024:(q + 1) * 1024],
                      f"{nb}0", writes=(f"{nb}0", f"{nb}1"), tag=f"{tag}_ldn")
                ACT(S[ab][:, 0:1024], S[nb][:, 0:1024], AF.Abs,
                    reads=(f"{nb}0", f"{nb}1"), writes=(f"{ab}0", f"{ab}1"),
                    tag=f"{tag}_abs")
                RED(S["sq0"][:, q:q + 1], S[ab][:, 0:1024],
                    reads=(f"{ab}0", f"{ab}1"), writes=("sq0",), tag=f"{tag}_red")
            if npieces > 1:
                RED(S["sq1"][:, 0:1], S["sq0"][:, 0:npieces],
                    reads=("sq0",), writes=("sq1",), tag=f"{tag}_red2")
                srcsum, srcr = S["sq1"][:, 0:1], ("sq1",)
            else:
                srcsum, srcr = S["sq0"][:, 0:1], ("sq0",)
            TS("vector", S[scale_nm][:, r:r + 1], srcsum, 1.0 / DD, OP.mult, 1e-5, OP.max,
               reads=srcr, writes=(f"{scale_nm}{r}",), tag=f"{tag}_sc")
            RECIP(S[rs_nm][:, r:r + 1], S[scale_nm][:, r:r + 1],
                  reads=(f"{scale_nm}{r}",), writes=(rs_nm,), tag=f"{tag}_rs")
        def fn_tp(E, out=PS[0][0:nch, 0:128], inp=S[rs_nm][:, 0:nch]):
            return E.transpose(out, inp, S["ident"][:])
        p.add("tensor", fn_tp, reads=(rs_nm, "ident"), writes=("pb0",), tag=f"{tag}_tp")
        bank_open[0] = False
        CP("vector", S["sq0"][0:nch, 0:128], PS[0][0:nch, 0:128],
           reads=("pb0",), writes=("sq0",), tag=f"{tag}_evt")
        p.dma("sync", rs_flat.rearrange("one (r p) -> one r p", r=nch).squeeze(0),
              S["sq0"][0:nch, 0:128], "qSt", reads=("sq0",), writes=(rsflat_nm,), tag=f"{tag}_st")
        assert EE <= NC_DI * T
        ab_cover = tuple(f"xmA{n}" for n in range(EE // T))
        p.dma("sync", S["xmainA"][:, 0:EE],
              rs_flat.unsqueeze(1).broadcast_to([1, 128, EE]),
              "qC", reads=(rsflat_nm,), writes=ab_cover, tag=f"{tag}_rep")
        kb_cover = tuple(f"xgA{n}" for n in range(1024 // T))
        for djj in range(DD // 128):
            for qq in range(EE // 1024):
                nb = "wbA" if (djj + qq) % 2 == 0 else "wbB"
                tb_ = "wbC" if (djj + qq) % 2 == 0 else "wbD"
                ab_r = tuple(f"xmA{n}" for n in range(qq * 4, qq * 4 + 4))
                tb2 = (f"{tb_}0", f"{tb_}1")
                nb2 = (f"{nb}0", f"{nb}1")
                p.dma("sync", S[nb][:, 0:1024],
                      tr[djj * 128:(djj + 1) * 128, qq * 1024:(qq + 1) * 1024],
                      f"{nb}0", writes=nb2, tag=f"{tag}_ldt")
                ACT(S[tb_][:, 0:1024], S[nb][:, 0:1024], AF.Abs,
                    reads=nb2, writes=tb2, tag=f"{tag}_ta")
                TT("vector", S[tb_][:, 0:1024], S[tb_][:, 0:1024],
                   S["xmainA"][:, qq * 1024:(qq + 1) * 1024], OP.mult,
                   reads=tb2 + ab_r, writes=tb2, tag=f"{tag}_tm")
                TS("vector", S[tb_][:, 0:1024], S[tb_][:, 0:1024], 0.5, OP.is_gt,
                   reads=tb2, writes=tb2, tag=f"{tag}_tg")
                ACT(S["xgateA"][:, 0:1024], S[nb][:, 0:1024], AF.Sign,
                    reads=nb2, writes=kb_cover, tag=f"{tag}_sg")
                TT("vector", S[tb_][:, 0:1024], S[tb_][:, 0:1024], S["xgateA"][:, 0:1024],
                   OP.mult, reads=tb2 + kb_cover, writes=tb2, tag=f"{tag}_tern")
                p.dma("sync", ternT[djj * 128:(djj + 1) * 128, qq * 1024:(qq + 1) * 1024],
                      S[tb_][:, 0:1024], "qSt", reads=tb2, writes=(ternT_nm,), tag=f"{tag}_stt")

    ternarize(wfc_n, wfc_t, HID, DM, "scale_fc", "rs_fc", rsfc_d, ternfcT,
              "fc_rsflat", "ternfcT", "fc")
    ternarize(wproj_n, wproj_t, DM, HID, "scale_pj", "rs_pj", rsproj_d, ternprojT,
              "pj_rsflat", "ternprojT", "pj")

    ALL_XG = tuple(f"xgA{c}" for c in range(NC_DI))
    ALL_XMA = tuple(f"xmA{c}" for c in range(NC_DI))
    p.dma("sync", S["xgateA"][0:NS, 0:HID], u_t, "qC", writes=ALL_XG, tag="us_ldu")
    p.dma("sync", S["xmainA"][0:NS, 0:HID],
          rsfc_d.unsqueeze(1).broadcast_to([1, NS, HID]),
          "qC", reads=("fc_rsflat",), writes=ALL_XMA, tag="us_ldrs")
    TT("vector", S["xgateA"][0:NS, 0:HID], S["xgateA"][0:NS, 0:HID],
       S["xmainA"][0:NS, 0:HID], OP.mult,
       reads=ALL_XG + ALL_XMA, writes=ALL_XG, tag="us_mul")
    p.dma("sync", us_d, S["xgateA"][0:NS, 0:HID], "qSt", reads=ALL_XG,
          writes=("us_d",), tag="us_st")

    STATR = S["statr"]
    mu_r = STATR[0:1, 0:T]; v1_r = STATR[0:1, T:2 * T]; w_r = STATR[0:1, 2 * T:3 * T]
    m2_r = w_r; sd_r = w_r; rstd_r = w_r; q_r = mu_r
    NH = NS // 2

    def ln_phase(xpre, xten, nch, scale_cols, bias_cols, scale_reads, outpre, outten, tag):
        for j in range(nch):
            sqb = "sq0" if j % 2 == 0 else "sq1"
            ACT(S[sqb][:], xten[:, j * T:(j + 1) * T], AF.Square,
                reads=(f"{xpre}{j}",), writes=(sqb,), tag=f"{tag}_sq{j}")
            MM(0, 1, S["ones_col"][:, 0:1], xten[:, j * T:(j + 1) * T],
               last_in_bank=(j == nch - 1), reads=(f"{xpre}{j}", "ones_col"),
               mslice=1, tag=f"{tag}_mmx{j}")
            MM(1, 1, S["ones_col"][:, 0:1], S[sqb][:],
               last_in_bank=(j == nch - 1), reads=(sqb, "ones_col"),
               mslice=1, tag=f"{tag}_mmq{j}")
        D = nch * 128
        TS("vector", mu_r, PS[0][0:1, T:2 * T], 1.0 / D, OP.mult,
           reads=("pb0",), writes=("mu_r",), tag=f"{tag}_mu")
        TS("vector", v1_r, PS[1][0:1, T:2 * T], 1.0 / D, OP.mult, EPS, OP.add,
           reads=("pb1",), writes=("v1_r",), tag=f"{tag}_v1")
        ACT(m2_r, mu_r, AF.Square, reads=("mu_r",), writes=("w_r",), tag=f"{tag}_m2")
        TT("vector", v1_r, v1_r, m2_r, OP.subtract,
           reads=("v1_r", "w_r"), writes=("v1_r",), tag=f"{tag}_var")
        ACT(sd_r, v1_r, AF.Sqrt, reads=("v1_r",), writes=("w_r",), tag=f"{tag}_sd")
        RECIP(rstd_r, sd_r, reads=("w_r",), writes=("w_r",), tag=f"{tag}_rstd")
        bcast_dma(S["rstd_rep"][:], rstd_r, "qLN", reads=("w_r",), writes=("rstd_rep",),
                  tag=f"{tag}_bc1")
        TT("vector", q_r, mu_r, rstd_r, OP.mult,
           reads=("mu_r", "w_r"), writes=("mu_r",), tag=f"{tag}_q")
        bcast_dma(S["q_rep"][:], q_r, "qLN", reads=("mu_r",), writes=("q_rep",), tag=f"{tag}_bc2")
        for j in range(nch):
            tm = "tmA" if j % 2 == 0 else "tmB"
            TT("vector", S[tm][:], xten[:, j * T:(j + 1) * T], S["rstd_rep"][:], OP.mult,
               reads=(f"{xpre}{j}", "rstd_rep"), writes=(tm,), tag=f"{tag}_s1{j}")
            TT("vector", S[tm][:], S[tm][:], S["q_rep"][:], OP.subtract,
               reads=(tm, "q_rep"), writes=(tm,), tag=f"{tag}_s2{j}")
            ACT(outten[:, j * T:(j + 1) * T], S[tm][:], AF.Identity,
                scale=scale_cols[:, j:j + 1], bias=bias_cols[:, j:j + 1],
                reads=(tm,) + scale_reads, writes=(f"{outpre}{j}",), tag=f"{tag}_ap{j}")

    def proj_phase(wdram, EE, rhs_ten, rhs_pre, nd, evac, tag, bufs=("wbA", "wbB"),
                   skip_djs=()):
        echunks = ceil_div(EE, 128)
        for g in range(ceil_div(echunks, 8)):
            e0 = g * 8
            ecnt = min(8, echunks - e0)
            ew = min(1024, EE - e0 * 128)
            for dj in range(nd):
                wb = bufs[dj % 2]
                for hh in range(2):
                    c0, c1 = hh * 512, min(ew, (hh + 1) * 512)
                    if c0 >= c1:
                        continue
                    if g == 0 and dj in skip_djs:
                        continue
                    p.dma("sync", S[wb][:, c0:c1],
                          wdram[dj * 128:(dj + 1) * 128, e0 * 128 + c0:e0 * 128 + c1],
                          f"{wb}{hh}", writes=(f"{wb}{hh}",),
                          tag=f"{tag}_ldw{g}_{dj}_{hh}")
                for j in range(ecnt):
                    m = min(128, EE - (e0 + j) * 128)
                    MM(j, 0, S[wb][:, j * 128:j * 128 + m],
                       rhs_ten[:, dj * T:(dj + 1) * T],
                       last_in_bank=(dj == nd - 1),
                       reads=(f"{wb}{j // 4}", f"{rhs_pre}{dj}"), mslice=m,
                       tag=f"{tag}_mm{g}_{dj}_{j}")
            for j in range(ecnt):
                evac(e0 + j, j)

    def xm(ti):
        return S["xmainA"] if ti % 2 == 0 else S["xmainB"]

    def xmp(ti):
        return "xmA" if ti % 2 == 0 else "xmB"

    def xg(ti):
        return S["xgateA"] if ti % 2 == 0 else S["xgateB"]

    def xgp(ti):
        return "xgA" if ti % 2 == 0 else "xgB"

    def A_ln(ti):
        t0 = ti * T
        xp = "x0_" if ti % 2 == 0 else "x1_"
        X = S["x0"] if ti % 2 == 0 else S["x1"]
        p.dma("sync", X[:].rearrange("p (j t) -> p j t", j=NC_DM),
              x_t.rearrange("(j p) l -> p j l", p=128)[:, :, t0:t0 + T],
              "qX", writes=tuple(f"{xp}{j}" for j in range(NC_DM)), tag=f"xin{ti}")
        ln_phase(xp, X, NC_DM, S["f1_c"], S["sa_c"], ("f1_c", "sa_c"),
                 "h", S["h"], f"t{ti}ln1")
        ln_phase("h", S["h"], NC_DM, S["lng_c"], S["lnb_c"], ("lng_c", "lnb_c"),
                 "xn", S["xn"], f"t{ti}ln2")
        for j in range(NC_DM):
            STT("vector", X[:, j * T:(j + 1) * T], S["h"][:, j * T:(j + 1) * T],
                S["ga_c"][:, j:j + 1], X[:, j * T:(j + 1) * T], OP.mult, OP.add,
                reads=(f"h{j}", "ga_c", f"{xp}{j}"), writes=(f"{xp}{j}",),
                tag=f"t{ti}xh{j}")

    def A_up1(ti):
        XM = xm(ti); xmn = xmp(ti)
        def evac_up1(ec, bank, ti=ti, XM=XM, xmn=xmn):
            CP("vector", S["Xmp"][:, KC - 1:KC - 1 + T], pslot(bank, 0),
               reads=(f"pb{bank}",), writes=("Xmp",), tag=f"t{ti}cv_ev{ec}")
            CP("vector", S["Xmp"][:, 0:KC - 1],
               S["convhalo"][:, ec * (KC - 1):(ec + 1) * (KC - 1)],
               reads=(f"convhalo{ec}",), writes=("Xmp",), tag=f"t{ti}cv_h{ec}")
            CP("vector", S["convhalo"][:, ec * (KC - 1):(ec + 1) * (KC - 1)],
               S["Xmp"][:, T:T + KC - 1],
               reads=("Xmp",), writes=(f"convhalo{ec}",), tag=f"t{ti}cv_sv{ec}")
            TS("vector", S["scA"][:], S["Xmp"][:, 0:T],
               S["convw_c"][:, ec * KC:ec * KC + 1], OP.mult,
               S["convb_c"][:, ec:ec + 1], OP.add,
               reads=("Xmp", "convw_c", "convb_c"), writes=("scA",), tag=f"t{ti}cv0_{ec}")
            for k in range(1, KC):
                STT("vector", S["scA"][:], S["Xmp"][:, k:k + T],
                    S["convw_c"][:, ec * KC + k:ec * KC + k + 1], S["scA"][:],
                    OP.mult, OP.add,
                    reads=("Xmp", "convw_c", "scA"), writes=("scA",), tag=f"t{ti}cv{k}_{ec}")
            ACT(XM[:, ec * T:(ec + 1) * T], S["scA"][:], AF.Silu,
                reads=("scA",), writes=(f"{xmn}{ec}",), tag=f"t{ti}cvs_{ec}")
        proj_phase(wup1_t, DI, S["xn"], "xn", NC_DM, evac_up1, f"t{ti}up1",
                   skip_djs=(0, 1))

    def A_up2(ti):
        XG = xg(ti); xgn = xgp(ti)
        def evac_up2(ec, bank, ti=ti, XG=XG, xgn=xgn):
            ACT(XG[:, ec * T:(ec + 1) * T], pslot(bank, 0), AF.Silu,
                reads=(f"pb{bank}",), writes=(f"{xgn}{ec}",), tag=f"t{ti}xg_{ec}")
        proj_phase(wup2_t, DI, S["xn"], "xn", NC_DM, evac_up2, f"t{ti}up2")

    def BmCm_phase(ti):
        XM = xm(ti); xmn = xmp(ti)
        for dj in range(NC_DI):
            wb = "wbA" if dj % 2 == 0 else "wbB"
            p.dma("sync", S[wb][:, 0:32], wll_t[dj * 128:(dj + 1) * 128, 2048:2080],
                  f"{wb}0", writes=(f"{wb}0",), tag=f"t{ti}bc_ldw{dj}")
            MM(2, 1, S[wb][:, 0:16], XM[:, dj * T:(dj + 1) * T],
               last_in_bank=(dj == NC_DI - 1), reads=(f"{wb}0", f"{xmn}{dj}"),
               mslice=16, tag=f"t{ti}bmm{dj}")
            MM(3, 1, S[wb][:, 16:32], XM[:, dj * T:(dj + 1) * T],
               last_in_bank=(dj == NC_DI - 1), reads=(f"{wb}0", f"{xmn}{dj}"),
               mslice=16, tag=f"t{ti}cmm{dj}")
        ACT(S["Bm"][:], PS[2][0:NS, T:2 * T], AF.Identity, bias=S["bll_c"][0:NS, 16:17],
            reads=("pb2", "bll_c"), writes=("Bm",), tag=f"t{ti}bm_ev")
        ACT(S["Cm"][:], PS[3][0:NS, T:2 * T], AF.Identity, bias=S["bll_c"][0:NS, 17:18],
            reads=("pb3", "bll_c"), writes=("Cm",), tag=f"t{ti}cm_ev")

    def reps_phase(ti):
        for n in range(NS):
            bcast_dma(S["Bm_rep"][:, n * T:(n + 1) * T], S["Bm"][n:n + 1, :],
                      f"qR{n % 4}", reads=("Bm",), writes=(f"Bmr{n}",), tag=f"t{ti}bmr{n}")
        for n in range(NS):
            bcast_dma(S["C_rep"][:, n * T:(n + 1) * T], S["Cm"][n:n + 1, :],
                      f"qR{n % 4}", reads=("Cm",), writes=(f"Cr{n}",), tag=f"t{ti}cr{n}")

    def pp_wave(ti, cs, banks):
        XM = xm(ti); xmn = xmp(ti)
        e0 = cs[0]
        ew = 128 * len(cs)
        for dj in range(NC_DI):
            wb = "wbA" if dj % 2 == 0 else "wbB"
            for hh in range(2):
                c0, c1 = hh * 512, min(ew, (hh + 1) * 512)
                if c0 >= c1:
                    continue
                p.dma("sync", S[wb][:, c0:c1],
                      wll_t[dj * 128:(dj + 1) * 128, e0 * 128 + c0:e0 * 128 + c1],
                      f"{wb}{hh}", writes=(f"{wb}{hh}",),
                      tag=f"t{ti}pp_ldw{e0}_{dj}_{hh}")
            for jj, ec in enumerate(cs):
                MM(banks[jj], 0, S[wb][:, jj * 128:(jj + 1) * 128],
                   XM[:, dj * T:(dj + 1) * T],
                   last_in_bank=(dj == NC_DI - 1),
                   reads=(f"{wb}{jj // 4}", f"{xmn}{dj}"), tag=f"t{ti}ppmm{ec}_{dj}")

    def sp_wave(ti, cs, banks):
        for jj, ec in enumerate(cs):
            j = banks[jj]
            db = ("deltaA", "deltaB", "deltaC", "deltaD")[ec % 4]
            ACT(S["scB"][:], pslot(j, 0), AF.Relu, bias=S["bll_c"][:, ec:ec + 1],
                reads=(f"pb{j}", "bll_c"), writes=("scB",), tag=f"t{ti}sp_z{ec}")
            ACT(S["tmA"][:], pslot(j, 0), AF.Abs, bias=S["bll_c"][:, ec:ec + 1],
                reads=(f"pb{j}", "bll_c"), writes=("tmA",), tag=f"t{ti}sp_a{ec}")
            ACT(S["tmB"][:], S["tmA"][:], AF.Exp, scale=S["kconst"][:, 0:1],
                reads=("tmA", "kconst"), writes=("tmB",), tag=f"t{ti}sp_e{ec}")
            ACT(S["tmA"][:], S["tmB"][:], AF.Ln, bias=S["kconst"][:, 1:2],
                reads=("tmB", "kconst"), writes=("tmA",), tag=f"t{ti}sp_l{ec}")
            TT("gpsimd", S[db][:], S["scB"][:], S["tmA"][:], OP.add,
               reads=("scB", "tmA"), writes=(db,), tag=f"t{ti}sp_d{ec}")

    def da_half(ti, c, hh):
        db = ("deltaA", "deltaB", "deltaC", "deltaD")[c % 4]
        H = NS // 2
        gh = 2 * c + hh
        AB = S["AbarA"] if gh % 2 == 0 else S["AbarB"]
        abn = tuple(f"Ab{gh % 2}s{j}" for j in range(H))
        n0 = hh * H
        def fn_da(E, AB=AB, c=c, n0=n0, db=db):
            return E.tensor_tensor(
                AB[:].rearrange("p (n t) -> p n t", n=H),
                bass.AP(S[db], 0, [[T, 128], [0, H], [1, T]]),
                bass.AP(S["A_fm"], c * NS + n0, [[NC_DI * NS, 128], [1, H], [0, T]]),
                OP.mult)
        p.add("gpsimd", fn_da, reads=(db, "A_fm"), writes=abn, tag=f"t{ti}c{c}h{hh}da")

    def exp_half(ti, c, hh):
        H = NS // 2
        gh = 2 * c + hh
        AB = S["AbarA"] if gh % 2 == 0 else S["AbarB"]
        abn = tuple(f"Ab{gh % 2}s{j}" for j in range(H))
        ACT(AB[:], AB[:], AF.Exp, reads=abn, writes=abn, tag=f"t{ti}c{c}h{hh}ex")

    def cube_chunk(ti, c):
        XM = xm(ti); xmn = xmp(ti)
        H = NS // 2
        exp_half(ti, c, 0); exp_half(ti, c, 1)
        if c < NC_DI - 1:
            nxt = (ti, c + 1)
        elif ti + 1 < NT:
            nxt = (ti + 1, 0)
        else:
            nxt = None
        for hh in range(2):
            gh = 2 * c + hh
            AB = S["AbarA"] if gh % 2 == 0 else S["AbarB"]
            abn = tuple(f"Ab{gh % 2}s{j}" for j in range(H))
            gbn = tuple(f"G_s{j}" for j in range(H))
            n0 = hh * H
            for j in range(H):
                n = n0 + j
                STT("vector", S["Gbuf"][:, j * T:(j + 1) * T],
                    XM[:, c * T:(c + 1) * T],
                    S["iA_fm"][:, c * NS + n:c * NS + n + 1],
                    S["Bm_rep"][:, n * T:(n + 1) * T], OP.mult, OP.mult,
                    reads=(f"{xmn}{c}", "iA_fm", f"Bmr{n}"), writes=(f"G_s{j}",),
                    tag=f"t{ti}c{c}g{n}")
            STT("vector", S["Gbuf"][:], AB[:], -1.0, S["Gbuf"][:], OP.add, OP.mult,
                reads=abn + gbn, writes=gbn, tag=f"t{ti}c{c}h{hh}u")
            for j in range(H):
                n = n0 + j
                def fn_sc(E, j=j, n=n, c=c, AB=AB):
                    return E.tensor_tensor_scan(
                        S["Gbuf"][:, j * T:(j + 1) * T],
                        S["Gbuf"][:, j * T:(j + 1) * T],
                        AB[:, j * T:(j + 1) * T],
                        S["K_state"][:, c * NS + n:c * NS + n + 1],
                        OP.add, OP.mult)
                p.add("vector", fn_sc, reads=(f"G_s{j}", abn[j], f"Kst{c}"),
                      writes=(f"G_s{j}",), tag=f"t{ti}c{c}sc{n}")
            def fn_svk(E, out=S["K_state"][:, c * NS + n0:c * NS + n0 + H],
                       inp=bass.AP(S["Gbuf"], T - 1, [[H * T, 128], [T, H]])):
                return E.tensor_copy(out, inp)
            p.add("vector", fn_svk, reads=gbn, writes=(f"Kst{c}",), tag=f"t{ti}c{c}h{hh}ks")
            if nxt is not None:
                da_half(nxt[0], nxt[1], hh)
            TT("vector", S["Gbuf"][:], S["Gbuf"][:],
               S["C_rep"][:, n0 * T:(n0 + H) * T], OP.mult,
               reads=gbn + tuple(f"Cr{n0 + j}" for j in range(H)),
               writes=gbn, tag=f"t{ti}c{c}h{hh}ch")
            red_out = S["yK"][:] if hh == 0 else S["t1b"][:]
            red_nm = "yK" if hh == 0 else "t1b"
            def fn_red(E, red_out=red_out):
                return E.tensor_reduce(
                    red_out,
                    bass.AP(S["Gbuf"], 0, [[H * T, 128], [1, T], [T, H]]),
                    AX.X, OP.add)
            p.add("vector", fn_red, reads=gbn, writes=(red_nm,), tag=f"t{ti}c{c}h{hh}red")
        TT("vector", S["yK"][:], S["yK"][:], S["t1b"][:], OP.add,
           reads=("yK", "t1b"), writes=("yK",), tag=f"t{ti}c{c}ry")
        TT("vector", xg(ti)[:, c * T:(c + 1) * T], S["yK"][:],
           xg(ti)[:, c * T:(c + 1) * T], OP.mult,
           reads=("yK", f"{xgp(ti)}{c}"), writes=(f"{xgp(ti)}{c}",), tag=f"t{ti}c{c}gt")

    def C1a_phase(ti):
        xp = "x0_" if ti % 2 == 0 else "x1_"
        X = S["x0"] if ti % 2 == 0 else S["x1"]
        XM = xm(ti)

        def evac_down(ec, bank, ti=ti, xp=xp, X=X):
            ACT(S["scB"][:], pslot(bank, 0), AF.Identity, bias=S["bdown_c"][:, ec:ec + 1],
                reads=(f"pb{bank}", "bdown_c"), writes=("scB",), tag=f"t{ti}dn_ev{ec}")
            STT("vector", X[:, ec * T:(ec + 1) * T], S["scB"][:],
                S["ga_c"][:, ec:ec + 1], X[:, ec * T:(ec + 1) * T], OP.mult, OP.add,
                reads=("scB", "ga_c", f"{xp}{ec}"), writes=(f"{xp}{ec}",), tag=f"t{ti}dn2{ec}")

        proj_phase(wdown_t, DM, xg(ti), xgp(ti), NC_DI, evac_down, f"t{ti}dn",
                   bufs=("wbC", "wbD"), skip_djs=(0, 1))

        ln_phase(xp, X, NC_DM, S["f2_c"], S["sb_c"], ("f2_c", "sb_c"),
                 "h", S["h"], f"t{ti}ln3")

        for dj in range(NC_DM):
            MM(2, 0, S["v_c"][:, dj * NS:(dj + 1) * NS], S["h"][:, dj * T:(dj + 1) * T],
               last_in_bank=(dj == NC_DM - 1), reads=(f"v_c{dj}", f"h{dj}"),
               mslice=NS, tag=f"t{ti}a1_{dj}")
        ACT(S["scA"][0:NS, :], PS[2][0:NS, 0:T], AF.Identity, reads=("pb2",),
            writes=("scA",), tag=f"t{ti}a1ev")
        MM(3, 0, S["fbgT"][:], S["scA"][0:NS, 0:T], last_in_bank=True,
           mslice=NS, reads=("fbgT", "scA"), tag=f"t{ti}a2")
        ACT(S["Xmp"][0:NS, 0:T], PS[3][0:NS, 0:T], AF.Identity, reads=("pb3",),
            writes=("Xmp",), tag=f"t{ti}a2ev")

    def uw_load(ti, k):
        ub = k % 4
        p.dma("sync", S["ubuf"][:, ub * 128:(ub + 1) * 128],
              us_d[:, k * 128:(k + 1) * 128], f"ubuf{ub % 2}",
              reads=("us_d",), writes=(f"ubuf{ub}",), tag=f"t{ti}uw{k}")

    def ffn_front(ti, k):
        fb = 0 if k % 2 == 0 else 1
        wfb = "wbA" if k % 2 == 0 else "wbB"
        wpb = "wbC" if k % 2 == 0 else "wbD"
        ub = k % 4
        for hh in range(2):
            p.dma("sync",
                  S[wfb][:].rearrange("p (j e) -> p j e", j=NC_DM)[:, hh * 4:hh * 4 + 4],
                  ternfcT.rearrange("(j p) e -> p j e", p=128)
                  [:, hh * 4:hh * 4 + 4, k * 128:(k + 1) * 128],
                  f"{wfb}{hh}", reads=("ternfcT",), writes=(f"{wfb}{hh}",),
                  tag=f"t{ti}fcw{k}_{hh}")
            p.dma("sync", S[wpb][:, hh * 512:(hh + 1) * 512],
                  ternprojT[k * 128:(k + 1) * 128, hh * 512:(hh + 1) * 512],
                  f"{wpb}{hh}", reads=("ternprojT",), writes=(f"{wpb}{hh}",),
                  tag=f"t{ti}pjw{k}_{hh}")
        if k + 2 < NC_HID:
            uw_load(ti, k + 2)
        MM(fb, 1, S["ubuf"][:, ub * 128:(ub + 1) * 128], S["Xmp"][0:NS, 0:T],
           last_in_bank=False, reads=(f"ubuf{ub}", "Xmp"), tag=f"t{ti}L{k}")
        for dj in range(NC_DM):
            MM(fb, 1, S[wfb][:, dj * 128:(dj + 1) * 128], S["h"][:, dj * T:(dj + 1) * T],
               last_in_bank=(dj == NC_DM - 1), reads=(f"{wfb}{dj // 4}", f"h{dj}"),
               tag=f"t{ti}F{k}_{dj}")

    def ffn_epi(ti, k):
        fb = 0 if k % 2 == 0 else 1
        pp_ = k % 2
        ACT(S[f"fA{pp_}"][:], pslot(fb, 1), AF.Relu, scale=S["scale_fc"][:, k:k + 1],
            reads=(f"pb{fb}", f"scale_fc{k}"), writes=(f"fA{pp_}",), tag=f"t{ti}tr{k}")
        ACT(S[f"fB{pp_}"][:], S[f"fA{pp_}"][:], AF.Square,
            reads=(f"fA{pp_}",), writes=(f"fB{pp_}",), tag=f"t{ti}r2{k}")

    def ffn_fc2(ti, k):
        pp_ = k % 2
        for j in range(NC_DM):
            bank = 4 + j // 2
            col = j % 2
            MM(bank, col, S["wbC" if k % 2 == 0 else "wbD"][:, j * 128:(j + 1) * 128],
               S[f"fB{pp_}"][:],
               last_in_bank=(k == NC_HID - 1 and j % 2 == 1),
               reads=(("wbC" if k % 2 == 0 else "wbD") + str(j // 4), f"fB{pp_}"),
               tag=f"t{ti}O{k}_{j}")

    def ffn_ks(ti, ks):
        for k in ks:
            if k == 0:
                uw_load(ti, 0); uw_load(ti, 1)
            ffn_front(ti, k)
            ffn_epi(ti, k)
            if k > 0:
                ffn_fc2(ti, k - 1)
            if k == NC_HID - 1:
                ffn_fc2(ti, k)

    def C2_phase(ti):
        t0 = ti * T
        xp = "x0_" if ti % 2 == 0 else "x1_"
        X = S["x0"] if ti % 2 == 0 else S["x1"]
        for j in range(NC_DM):
            bank = 4 + j // 2
            col = j % 2
            ACT(S["scA"][:], pslot(bank, col), AF.Identity, scale=S["scale_pj"][:, j:j + 1],
                reads=(f"pb{bank}", f"scale_pj{j}"), writes=("scA",), tag=f"t{ti}fo_e{j}")
            STT("vector", X[:, j * T:(j + 1) * T], S["scA"][:], S["gb_c"][:, j:j + 1],
                X[:, j * T:(j + 1) * T], OP.mult, OP.add,
                reads=("scA", "gb_c", f"{xp}{j}"), writes=(f"{xp}{j}",), tag=f"t{ti}fo_x{j}")
        p.dma("scalar", out_t.rearrange("(j p) l -> p j l", p=128)[:, :, t0:t0 + T],
              X[:].rearrange("p (j t) -> p j t", j=NC_DM),
              "qOut", reads=tuple(f"{xp}{j}" for j in range(NC_DM)), tag=f"xout{ti}")

    def dn_prefetch(ti):
        for dj in (0, 1):
            wb = ("wbC", "wbD")[dj % 2]
            for hh in range(2):
                p.dma("sync", S[wb][:, hh * 512:(hh + 1) * 512],
                      wdown_t[dj * 128:(dj + 1) * 128, hh * 512:(hh + 1) * 512],
                      f"{wb}{hh}", writes=(f"{wb}{hh}",), tag=f"t{ti}dn_pre{dj}_{hh}")

    def up1_prefetch(ti):
        for dj in (0, 1):
            wb = ("wbA", "wbB")[dj % 2]
            for hh in range(2):
                p.dma("sync", S[wb][:, hh * 512:(hh + 1) * 512],
                      wup1_t[dj * 128:(dj + 1) * 128, hh * 512:(hh + 1) * 512],
                      f"{wb}{hh}", writes=(f"{wb}{hh}",), tag=f"t{ti}up1_pre{dj}_{hh}")

    def sp1(ti, c, b):
        sp_wave(ti, [c], [b])

    A_ln(0); up1_prefetch(0); A_up1(0); A_up2(0)
    BmCm_phase(0)
    reps_phase(0)
    pp_wave(0, list(range(0, 8)), list(range(8)))
    sp1(0, 0, 0); sp1(0, 1, 1); sp1(0, 2, 2); sp1(0, 3, 3)
    da_half(0, 0, 0); da_half(0, 0, 1)
    for ti in range(NT):
        ffn = (lambda ks: ffn_ks(ti - 1, ks)) if ti > 0 else (lambda ks: None)
        cube_chunk(ti, 0); sp1(ti, 4, 4)
        cube_chunk(ti, 1); sp1(ti, 5, 5)
        if ti > 0:
            C1a_phase(ti - 1)
        cube_chunk(ti, 2); sp1(ti, 6, 6)
        cube_chunk(ti, 3); sp1(ti, 7, 7)
        pp_wave(ti, [8, 9], [2, 3])
        cube_chunk(ti, 4); sp1(ti, 8, 2); ffn(range(0, 4))
        cube_chunk(ti, 5); sp1(ti, 9, 3); ffn(range(4, 8))
        pp_wave(ti, [10, 11], [2, 3])
        cube_chunk(ti, 6); sp1(ti, 10, 2); ffn(range(8, 12))
        cube_chunk(ti, 7); sp1(ti, 11, 3); ffn(range(12, 16))
        cube_chunk(ti, 8); ffn(range(16, 20))
        cube_chunk(ti, 9); ffn(range(20, 24))
        pp_wave(ti, [12, 13, 14, 15], [0, 1, 2, 3])
        sp1(ti, 12, 0); sp1(ti, 13, 1)
        cube_chunk(ti, 10); ffn(range(24, 28))
        cube_chunk(ti, 11); ffn(range(28, 32))
        if ti + 1 < NT:
            up1_prefetch(ti + 1)
        if ti > 0:
            C2_phase(ti - 1)
        sp1(ti, 14, 2); sp1(ti, 15, 3)
        if ti + 1 < NT:
            A_ln(ti + 1)
            A_up1(ti + 1)
        cube_chunk(ti, 12)
        cube_chunk(ti, 13)
        if ti + 1 < NT:
            A_up2(ti + 1)
            BmCm_phase(ti + 1)
            pp_wave(ti + 1, list(range(0, 8)), list(range(8)))
            sp1(ti + 1, 0, 0); sp1(ti + 1, 1, 1)
        cube_chunk(ti, 14)
        if ti + 1 < NT:
            sp1(ti + 1, 2, 2)
        if ti + 1 < NT:
            dn_prefetch(ti)
        cube_chunk(ti, 15)
        if ti + 1 < NT:
            reps_phase(ti + 1)
            sp1(ti + 1, 3, 3)
    dn_prefetch(NT - 1)
    C1a_phase(NT - 1)
    ffn_ks(NT - 1, range(NC_HID))
    C2_phase(NT - 1)

    _LAST_PLAN[0] = p
    p.finalize()
    with contextlib.ExitStack() as st:
        sems = {}
        for name in sorted(p.final_counts):
            sems[name] = st.enter_context(nc.semaphore(name))
        block = st.enter_context(nc.Block())
        p.emit(block, sems)

        fc = dict(p.final_counts)
        def fin_scalar(E, fc=fc):
            if fc.get("d_qOut"):
                E.wait_ge(sems["d_qOut"], fc["d_qOut"])
        def fin_sync(E, fc=fc):
            for k in sorted(fc):
                if k.startswith("d_") and k != "d_qOut" and fc[k]:
                    E.wait_ge(sems[k], fc[k])
        block.sync(fin_sync)
        block.scalar(fin_scalar)
    return nc


def _prep_core_inputs(b, I):
    c = np.ascontiguousarray
    fm = lambda v, nch: c(np.asarray(v, np.float32).reshape(nch, 128).T)
    bll_v = np.asarray(I["L0_attn_b_ll"], np.float32)
    bll_p = np.zeros(18 * 128, np.float32)
    bll_p[:2048] = bll_v[:2048]
    bll_p[16 * 128:16 * 128 + 16] = bll_v[2048:2064]
    bll_p[17 * 128:17 * 128 + 16] = bll_v[2064:2080]
    convw = np.asarray(I["L0_attn_conv_w"], np.float32)[:, 0, :]
    alog = np.asarray(I["L0_attn_A_log"], np.float32)
    m = {
        "x_t": c(np.asarray(I["L0_attn_X"][b], np.float32).T),
        "wup1_t": c(np.asarray(I["L0_attn_W_up1"], np.float32).T),
        "wup2_t": c(np.asarray(I["L0_attn_W_up2"], np.float32).T),
        "wll_t": c(np.asarray(I["L0_attn_W_ll"], np.float32).T),
        "wdown_t": c(np.asarray(I["L0_attn_W_down"], np.float32).T),
        "wfc_t": c(np.asarray(I["L0_ffn_w_fc"], np.float32).T),
        "wfc_n": c(np.asarray(I["L0_ffn_w_fc"], np.float32)),
        "wproj_t": c(np.asarray(I["L0_ffn_w_proj"], np.float32).T),
        "wproj_n": c(np.asarray(I["L0_ffn_w_proj"], np.float32)),
        "u_t": c(np.asarray(I["L0_ffn_u"], np.float32).T),
        "v_t": c(np.asarray(I["L0_ffn_v"], np.float32).T),
        "fastb_t": c(np.asarray(I["L0_ffn_fast_b"], np.float32).T),
        "wmod": c(np.asarray(I["L0_wmod"], np.float32)),
        "cfm": fm(I["L0_c"][b], 8),
        "lng": fm(I["L0_attn_ln_g"], 8),
        "lnb": fm(I["L0_attn_ln_b"], 8),
        "bll": fm(bll_p, 18),
        "bdown": fm(I["L0_attn_b_down"], 8),
        "convw": c(convw.reshape(16, 128, 4).transpose(1, 0, 2).reshape(128, 64)),
        "convb": fm(I["L0_attn_conv_b"], 16),
        "alog_fm": c(alog.reshape(16, 128, 16).transpose(1, 0, 2).reshape(128, 256)),
        "gate_in": np.asarray(I["L0_ffn_gate"], np.float32).reshape(1, 1),
        "ident_in": np.eye(128, dtype=np.float32),
    }
    return {k: np.ascontiguousarray(v, dtype=np.float32) for k, v in m.items()}


def kernel(**inputs):
    nc = build_program(NT=L_FULL // T)
    in_maps = [_prep_core_inputs(b, inputs) for b in range(NCHIP)]
    res = run_bass_kernel_spmd(nc, in_maps, core_ids=list(range(NCHIP)))
    out = np.stack([np.asarray(r["out_t"]).T for r in res.results], axis=0)
    return np.ascontiguousarray(out, dtype=np.float32)


# version 27
# speedup vs baseline: 1.0233x; 1.0233x over previous
import contextlib
import numpy as np

import concourse.bass as bass
import concourse.mybir as mybir
from concourse.alu_op_type import AluOpType as OP
from concourse.bass_utils import run_bass_kernel_spmd

F32 = mybir.dt.float32
AF = mybir.ActivationFunctionType
AX = mybir.AxisListType

DM = 1024
DI = 2048
NS = 16
HID = 4096
KC = 4
T = 256
L_FULL = 4096
NCHIP = 8
EPS = 1e-5

NC_DM = DM // 128
NC_DI = DI // 128
NC_HID = HID // 128

ENGINES = ("sync", "tensor", "vector", "scalar", "gpsimd")
SAME_ENGINE_SYNC = True
BENCH = 0


class _Op:
    __slots__ = ("eng", "fn", "dma", "stream", "deps", "drain", "token", "tag", "idx")

    def __init__(self, eng, fn, dma, tag, stream=None):
        self.eng, self.fn, self.dma, self.tag = eng, fn, dma, tag
        self.stream = stream
        self.deps = []
        self.drain = False
        self.token = None
        self.idx = 0


class Plan:

    def __init__(self):
        self.q = {e: [] for e in ENGINES}
        self.last_w = {}
        self.readers = {}
        self.all_ops = []
        self.stream_last = {}
        self.stream_cnt = {}

    def add(self, eng, fn, reads=(), writes=(), dma=False, tag="", stream=None):
        op = _Op(eng, fn, dma, tag, stream)
        deps = []

        def dep_on(d):
            if d is None or d is op:
                return
            if d.dma or d.eng != eng:
                if d not in deps:
                    deps.append(d)
            elif SAME_ENGINE_SYNC and eng != "tensor":
                if d not in deps:
                    deps.append(d)

        for r in reads:
            dep_on(self.last_w.get(r))
        for w in writes:
            dep_on(self.last_w.get(w))
            for rd in self.readers.get(w, ()):
                dep_on(rd)
        op.deps = deps
        for r in reads:
            self.readers.setdefault(r, []).append(op)
        for w in writes:
            self.last_w[w] = op
            self.readers[w] = []
        if dma:
            prev = self.stream_last.get(stream)
            if prev is not None and prev not in deps:
                deps.append(prev)
            self.stream_last[stream] = op
            self.stream_cnt[stream] = self.stream_cnt.get(stream, 0) + 16
            op.token = ("d_" + stream, self.stream_cnt[stream])
        op.idx = len(self.all_ops)
        self.all_ops.append(op)
        self.q[eng].append(op)
        return op

    def dma(self, eng, out, in_, stream, reads=(), writes=(), tag=""):
        def fn(E, out=out, in_=in_):
            return E.dma_start(out=out, in_=in_)
        return self.add(eng, fn, reads=reads, writes=writes, dma=True, tag=tag,
                        stream=stream)

    def finalize(self):
        eng_cnt = {e: 0 for e in ENGINES}
        needed = set()
        for o in self.all_ops:
            for d in o.deps:
                if not d.dma:
                    needed.add(id(d))
        for e in ENGINES:
            for o in self.q[e]:
                if not o.dma and id(o) in needed:
                    eng_cnt[e] += 1
                    o.token = ("s_" + e, eng_cnt[e])
        self.final_counts = {"d_" + k: v for k, v in self.stream_cnt.items()}
        for e in ENGINES:
            self.final_counts["s_" + e] = eng_cnt[e]
        for o in self.all_ops:
            for d in o.deps:
                assert d.idx < o.idx, f"dep cycle: {d.tag} -> {o.tag}"

    def emit(self, block, sems):
        watermark = {e: {} for e in ENGINES}

        def run(E, eng):
            for o in self.q[eng]:
                want = {}
                for d in o.deps:
                    s, v = d.token
                    if want.get(s, 0) < v:
                        want[s] = v
                for s, v in sorted(want.items()):
                    if watermark[eng].get(s, 0) >= v:
                        continue
                    E.wait_ge(sems[s], v)
                    watermark[eng][s] = v
                ins = o.fn(E)
                if o.token is not None:
                    s, _ = o.token
                    ins.then_inc(sems[s], 16 if o.dma else 1)

        block.sync(lambda E: run(E, "sync"))
        block.tensor(lambda E: run(E, "tensor"))
        block.vector(lambda E: run(E, "vector"))
        block.scalar(lambda E: run(E, "scalar"))
        block.gpsimd(lambda E: run(E, "gpsimd"))


def ceil_div(a, b):
    return -(-a // b)


_LAST_PLAN = [None]


def build_program(NT=16):
    L = NT * T
    nc = bass.Bass("TRN2", target_bir_lowering=False)
    p = Plan()

    d_in = {}
    def din(name, shape):
        d_in[name] = nc.dram_tensor(name, shape, F32, kind="ExternalInput").ap()
        return d_in[name]

    x_t = din("x_t", [DM, L])
    wup1_t = din("wup1_t", [DM, DI])
    wup2_t = din("wup2_t", [DM, DI])
    wll_t = din("wll_t", [DI, 2080])
    wdown_t = din("wdown_t", [DI, DM])
    wfc_t = din("wfc_t", [DM, HID])
    wfc_n = din("wfc_n", [HID, DM])
    wproj_t = din("wproj_t", [HID, DM])
    wproj_n = din("wproj_n", [DM, HID])
    u_t = din("u_t", [NS, HID])
    v_t = din("v_t", [DM, NS])
    fastb_t = din("fastb_t", [NS, NS])
    wmod = din("wmod", [DM, 6 * DM])
    din("cfm", [128, NC_DM])
    din("lng", [128, NC_DM])
    din("lnb", [128, NC_DM])
    din("bll", [128, 18])
    din("bdown", [128, NC_DM])
    din("convw", [128, NC_DI * KC])
    din("convb", [128, NC_DI])
    din("alog_fm", [128, NC_DI * NS])
    gate_in = din("gate_in", [1, 1])
    din("ident_in", [128, 128])

    out_t = nc.dram_tensor("out_t", [DM, L], F32, kind="ExternalOutput").ap()

    ternfcT = nc.dram_tensor("ternfcT", [DM, HID], F32, kind="Internal").ap()
    ternprojT = nc.dram_tensor("ternprojT", [HID, DM], F32, kind="Internal").ap()
    rsfc_d = nc.dram_tensor("rsfc_d", [1, HID], F32, kind="Internal").ap()
    us_d = nc.dram_tensor("us_d", [NS, HID], F32, kind="Internal").ap()
    rsproj_d = nc.dram_tensor("rsproj_d", [1, DM], F32, kind="Internal").ap()

    S = {}
    def sbt(name, shape, dtype=F32):
        S[name] = nc.alloc_sbuf_tensor(name, shape, dtype)
        return S[name]

    sbt("ident", [128, 128])
    for nm, w in [("lng_c", NC_DM), ("lnb_c", NC_DM), ("bdown_c", NC_DM),
                  ("convb_c", NC_DI), ("silu_c", NC_DM), ("cfm_c", NC_DM)]:
        sbt(nm, [128, w])
    sbt("bll_c", [128, 18])
    sbt("convw_c", [128, NC_DI * KC])
    sbt("A_fm", [128, NC_DI * NS])
    sbt("iA_fm", [128, NC_DI * NS])
    sbt("mod_c", [128, 48])
    for nm in ("f1_c", "sa_c", "ga_c", "f2_c", "sb_c", "gb_c"):
        sbt(nm, [128, NC_DM])
    sbt("scale_fc", [128, NC_HID]); sbt("scale_pj", [128, NC_DM])
    sbt("rs_fc", [128, NC_HID]); sbt("rs_pj", [128, NC_DM])
    sbt("v_c", [128, NC_DM * NS])
    sbt("fbgT", [NS, NS])
    sbt("gate_c", [NS, 1])
    sbt("ones_col", [128, 1])
    sbt("kconst", [128, 2])

    sbt("wbA", [128, 1024]); sbt("wbB", [128, 1024])
    sbt("wbC", [128, 1024]); sbt("wbD", [128, 1024])
    sbt("ubuf", [NS, 4 * 128])

    sbt("x0", [128, NC_DM * T]); sbt("x1", [128, NC_DM * T])
    sbt("h", [128, NC_DM * T])
    sbt("xn", [128, NC_DM * T])
    sbt("xmainA", [128, NC_DI * T]); sbt("xmainB", [128, NC_DI * T])
    sbt("xgateA", [128, NC_DI * T]); sbt("xgateB", [128, NC_DI * T])
    for _dn in ("deltaA", "deltaB", "deltaC", "deltaD"):
        sbt(_dn, [128, T])
    sbt("Bm", [NS, T]); sbt("Cm", [NS, T])
    sbt("Bm_rep", [128, NS * T])
    sbt("C_rep", [128, NS * T])
    sbt("AbarA", [128, (NS // 2) * T]); sbt("AbarB", [128, (NS // 2) * T])
    sbt("Gbuf", [128, (NS // 2) * T])
    sbt("K_state", [128, NC_DI * NS])
    sbt("convhalo", [128, NC_DI * (KC - 1)])
    sbt("Xmp", [128, T + KC - 1])
    sbt("yK", [128, T]); sbt("t1b", [128, T])
    sbt("sq0", [128, T]); sbt("sq1", [128, T])
    sbt("tmA", [128, T]); sbt("tmB", [128, T])
    sbt("scA", [128, T]); sbt("scB", [128, T])
    sbt("statr", [1, 3 * T])
    sbt("rstd_rep", [128, T]); sbt("q_rep", [128, T])
    for _fn in ("fA0", "fA1", "fB0", "fB1"):
        sbt(_fn, [128, T])

    PS = [nc.alloc_psum_tensor(f"pb{i}", [128, 512], F32) for i in range(8)]

    def pslot(b, c):
        return PS[b][:, c * T:(c + 1) * T]

    bank_open = [False] * 8

    def MM(bank, col, lhsT, rhs, *, last_in_bank, reads, tag="mm", mslice=None,
           ncols=T):
        st = not bank_open[bank]
        bank_open[bank] = not last_in_bank
        m = 128 if mslice is None else mslice
        outap = PS[bank][0:m, col * T:col * T + ncols]

        def fn(E, outap=outap, lhsT=lhsT, rhs=rhs, st=st, sp=last_in_bank):
            return E.matmul(outap, lhsT, rhs, start=st, stop=sp)
        return p.add("tensor", fn, reads=reads, writes=(f"pb{bank}",), tag=tag)

    def ACT(outap, inap, func, *, bias=0.0, scale=1.0, reads, writes, tag="act"):
        def fn(E, outap=outap, inap=inap, func=func, bias=bias, scale=scale):
            return E.activation(outap, inap, func, bias=bias, scale=scale)
        return p.add("scalar", fn, reads=reads, writes=writes, tag=tag)

    def TT(eng, outap, a, b, op, *, reads, writes, tag="tt"):
        def fn(E, outap=outap, a=a, b=b, op=op):
            return E.tensor_tensor(outap, a, b, op)
        return p.add(eng, fn, reads=reads, writes=writes, tag=tag)

    def TS(eng, outap, a, s1, op0, s2=None, op1=OP.bypass, *, reads, writes, tag="ts"):
        def fn(E, outap=outap, a=a, s1=s1, s2=s2, op0=op0, op1=op1):
            return E.tensor_scalar(outap, a, s1, s2, op0, op1)
        return p.add(eng, fn, reads=reads, writes=writes, tag=tag)

    def STT(eng, outap, in0, scal, in1, op0, op1, *, reads, writes, tag="stt"):
        def fn(E, outap=outap, in0=in0, scal=scal, in1=in1, op0=op0, op1=op1):
            return E.scalar_tensor_tensor(outap, in0, scal, in1, op0, op1)
        return p.add(eng, fn, reads=reads, writes=writes, tag=tag)

    def CP(eng, outap, inap, *, reads, writes, tag="cp"):
        def fn(E, outap=outap, inap=inap):
            return E.tensor_copy(outap, inap)
        return p.add(eng, fn, reads=reads, writes=writes, tag=tag)

    def RED(outap, inap, *, reads, writes, tag="red"):
        def fn(E, outap=outap, inap=inap):
            return E.tensor_reduce(outap, inap, AX.X, OP.add)
        return p.add("vector", fn, reads=reads, writes=writes, tag=tag)

    def RECIP(outap, inap, *, reads, writes, tag="recip"):
        def fn(E, outap=outap, inap=inap):
            return E.reciprocal(outap, inap)
        return p.add("vector", fn, reads=reads, writes=writes, tag=tag)

    def MEMSET(ap_, val, *, writes, tag="memset"):
        def fn(E, ap_=ap_, val=val):
            return E.memset(ap_, val)
        return p.add("vector", fn, writes=writes, tag=tag)

    def bcast_dma(outap, src_row_ap, stream, *, reads, writes, tag="bc"):
        ap = src_row_ap.unsqueeze(1).broadcast_to([1, 128, src_row_ap.shape[-1]])
        return p.dma("sync", outap, ap, stream, reads=reads, writes=writes, tag=tag)

    ALL_AB = tuple(f"Ab{n}" for n in range(NS))

    MEMSET(S["ones_col"][:], 1.0, writes=("ones_col",))
    MEMSET(S["kconst"][:, 0:1], -1.0, writes=("kconst",))
    MEMSET(S["kconst"][:, 1:2], 1.0, writes=("kconst",))
    MEMSET(S["K_state"][:], 0.0, writes=tuple(f"Kst{c}" for c in range(NC_DI)))
    MEMSET(S["convhalo"][:], 0.0, writes=tuple(f"convhalo{c}" for c in range(NC_DI)))

    if BENCH:
        def fn_b(E, nn=None):
            return E.matmul(PS[0][:, 0:nn], S["ident"][:], S["Bm_rep"][:, 0:nn],
                            start=True, stop=True)
        for rr in range(200):
            p.add("tensor", (lambda E, nn=512: fn_b(E, nn)), reads=("ident",),
                  writes=("pb0",), tag=f"bench512_{rr}")
        for rr in range(200):
            p.add("tensor", (lambda E, nn=256: fn_b(E, nn)), reads=("ident",),
                  writes=("pb0",), tag=f"bench256_{rr}")
        for rr in range(200):
            p.add("tensor", (lambda E, nn=384: fn_b(E, nn)), reads=("ident",),
                  writes=("pb0",), tag=f"bench384_{rr}")

    for nm, src in [("ident", "ident_in"), ("lng_c", "lng"), ("lnb_c", "lnb"),
                    ("bdown_c", "bdown"), ("convb_c", "convb"), ("bll_c", "bll"),
                    ("convw_c", "convw"), ("cfm_c", "cfm")]:
        p.dma("sync", S[nm][:], d_in[src], "qC", writes=(nm,), tag=f"ld_{nm}")
    p.dma("sync", S["A_fm"][:], d_in["alog_fm"], "qC", writes=("A_fm",), tag="ld_afm")
    p.dma("sync", S["fbgT"][:], fastb_t, "qC", writes=("fbgT",), tag="ld_fbt")
    for j in range(NC_DM):
        p.dma("sync", S["v_c"][:, j * NS:(j + 1) * NS], v_t[j * 128:(j + 1) * 128, :],
              "qC", writes=(f"v_c{j}",), tag="ld_v")
    p.dma("sync", S["gate_c"][:],
          gate_in[0:1, 0:1].unsqueeze(1).broadcast_to([1, NS, 1]),
          "qC", writes=("gate_c",), tag="ld_gate")

    ACT(S["iA_fm"][:], S["A_fm"][:], AF.Exp, reads=("A_fm",), writes=("iA_fm",), tag="a_exp")
    TS("vector", S["A_fm"][:], S["iA_fm"][:], -1.0, OP.mult,
       reads=("iA_fm",), writes=("A_fm",), tag="a_neg")
    RECIP(S["iA_fm"][:], S["A_fm"][:], reads=("A_fm",), writes=("iA_fm",), tag="a_recip")
    ACT(S["silu_c"][:], S["cfm_c"][:], AF.Silu, reads=("cfm_c",), writes=("silu_c",), tag="siluc")

    for g in range(6):
        for dj in range(NC_DM):
            wb = "wbA" if (g * NC_DM + dj) % 2 == 0 else "wbB"
            p.dma("sync", S[wb][:],
                  wmod[dj * 128:(dj + 1) * 128, g * 1024:(g + 1) * 1024],
                  f"{wb}0", writes=(f"{wb}0", f"{wb}1"), tag="ld_wmod")
            for j in range(8):
                MM(j, 0, S[wb][:, j * 128:(j + 1) * 128], S["silu_c"][:, dj:dj + 1],
                   last_in_bank=(dj == NC_DM - 1), reads=(f"{wb}{j // 4}", "silu_c"),
                   ncols=1, tag=f"mm_mod{g}_{dj}_{j}")
        for j in range(8):
            ec = g * 8 + j
            ACT(S["mod_c"][:, ec:ec + 1], PS[j][:, 0:1], AF.Copy,
                reads=(f"pb{j}",), writes=("mod_c",), tag=f"ev_mod{ec}")

    mc = S["mod_c"]
    CP("vector", S["sa_c"][:], mc[:, 0:8], reads=("mod_c",), writes=("sa_c",), tag="sa")
    TS("vector", S["f1_c"][:], mc[:, 8:16], 1.0, OP.add, reads=("mod_c",), writes=("f1_c",), tag="f1")
    CP("vector", S["sb_c"][:], mc[:, 24:32], reads=("mod_c",), writes=("sb_c",), tag="sbv")
    TS("vector", S["f2_c"][:], mc[:, 32:40], 1.0, OP.add, reads=("mod_c",), writes=("f2_c",), tag="f2")
    for src_off, dst in [(16, "ga_c"), (40, "gb_c")]:
        TS("vector", S["tmA"][:, 0:8], mc[:, src_off:src_off + 8], 0.0, OP.is_ge, 2.0, OP.mult,
           reads=("mod_c",), writes=("tmA",), tag="g_s0")
        TS("vector", S["tmA"][:, 0:8], S["tmA"][:, 0:8], -1.0, OP.add,
           reads=("tmA",), writes=("tmA",), tag="g_s1")
        ACT(S["tmB"][:, 0:8], mc[:, src_off:src_off + 8], AF.Abs,
            reads=("mod_c",), writes=("tmB",), tag="g_abs")
        TS("vector", S["tmB"][:, 0:8], S["tmB"][:, 0:8], 0.25, OP.max,
           reads=("tmB",), writes=("tmB",), tag="g_max")
        TT("vector", S[dst][:], S["tmA"][:, 0:8], S["tmB"][:, 0:8], OP.mult,
           reads=("tmA", "tmB"), writes=(dst,), tag="g_mul")

    def fn_fbg(E):
        return E.tensor_scalar(S["fbgT"][:], S["fbgT"][:], S["gate_c"][:, 0:1],
                               None, OP.mult, OP.bypass)
    p.add("vector", fn_fbg, reads=("fbgT", "gate_c"), writes=("fbgT",), tag="fbg")

    def ternarize(nat, tr, EE, DD, scale_nm, rs_nm, rs_flat, ternT, rsflat_nm, ternT_nm, tag):
        nch = EE // 128
        npieces = DD // 1024
        for r in range(nch):
            nb = "wbA" if r % 2 == 0 else "wbB"
            ab = "wbC" if r % 2 == 0 else "wbD"
            for q in range(npieces):
                p.dma("sync", S[nb][:, 0:1024],
                      nat[r * 128:(r + 1) * 128, q * 1024:(q + 1) * 1024],
                      f"{nb}0", writes=(f"{nb}0", f"{nb}1"), tag=f"{tag}_ldn")
                ACT(S[ab][:, 0:1024], S[nb][:, 0:1024], AF.Abs,
                    reads=(f"{nb}0", f"{nb}1"), writes=(f"{ab}0", f"{ab}1"),
                    tag=f"{tag}_abs")
                RED(S["sq0"][:, q:q + 1], S[ab][:, 0:1024],
                    reads=(f"{ab}0", f"{ab}1"), writes=("sq0",), tag=f"{tag}_red")
            if npieces > 1:
                RED(S["sq1"][:, 0:1], S["sq0"][:, 0:npieces],
                    reads=("sq0",), writes=("sq1",), tag=f"{tag}_red2")
                srcsum, srcr = S["sq1"][:, 0:1], ("sq1",)
            else:
                srcsum, srcr = S["sq0"][:, 0:1], ("sq0",)
            TS("vector", S[scale_nm][:, r:r + 1], srcsum, 1.0 / DD, OP.mult, 1e-5, OP.max,
               reads=srcr, writes=(f"{scale_nm}{r}",), tag=f"{tag}_sc")
            RECIP(S[rs_nm][:, r:r + 1], S[scale_nm][:, r:r + 1],
                  reads=(f"{scale_nm}{r}",), writes=(rs_nm,), tag=f"{tag}_rs")
        def fn_tp(E, out=PS[0][0:nch, 0:128], inp=S[rs_nm][:, 0:nch]):
            return E.transpose(out, inp, S["ident"][:])
        p.add("tensor", fn_tp, reads=(rs_nm, "ident"), writes=("pb0",), tag=f"{tag}_tp")
        bank_open[0] = False
        CP("vector", S["sq0"][0:nch, 0:128], PS[0][0:nch, 0:128],
           reads=("pb0",), writes=("sq0",), tag=f"{tag}_evt")
        p.dma("sync", rs_flat.rearrange("one (r p) -> one r p", r=nch).squeeze(0),
              S["sq0"][0:nch, 0:128], "qSt", reads=("sq0",), writes=(rsflat_nm,), tag=f"{tag}_st")
        assert EE <= NC_DI * T
        ab_cover = tuple(f"xmA{n}" for n in range(EE // T))
        p.dma("sync", S["xmainA"][:, 0:EE],
              rs_flat.unsqueeze(1).broadcast_to([1, 128, EE]),
              "qC", reads=(rsflat_nm,), writes=ab_cover, tag=f"{tag}_rep")
        kb_cover = tuple(f"xgA{n}" for n in range(1024 // T))
        for djj in range(DD // 128):
            for qq in range(EE // 1024):
                nb = "wbA" if (djj + qq) % 2 == 0 else "wbB"
                tb_ = "wbC" if (djj + qq) % 2 == 0 else "wbD"
                ab_r = tuple(f"xmA{n}" for n in range(qq * 4, qq * 4 + 4))
                tb2 = (f"{tb_}0", f"{tb_}1")
                nb2 = (f"{nb}0", f"{nb}1")
                p.dma("sync", S[nb][:, 0:1024],
                      tr[djj * 128:(djj + 1) * 128, qq * 1024:(qq + 1) * 1024],
                      f"{nb}0", writes=nb2, tag=f"{tag}_ldt")
                ACT(S[tb_][:, 0:1024], S[nb][:, 0:1024], AF.Abs,
                    reads=nb2, writes=tb2, tag=f"{tag}_ta")
                TT("vector", S[tb_][:, 0:1024], S[tb_][:, 0:1024],
                   S["xmainA"][:, qq * 1024:(qq + 1) * 1024], OP.mult,
                   reads=tb2 + ab_r, writes=tb2, tag=f"{tag}_tm")
                TS("vector", S[tb_][:, 0:1024], S[tb_][:, 0:1024], 0.5, OP.is_gt,
                   reads=tb2, writes=tb2, tag=f"{tag}_tg")
                ACT(S["xgateA"][:, 0:1024], S[nb][:, 0:1024], AF.Sign,
                    reads=nb2, writes=kb_cover, tag=f"{tag}_sg")
                TT("vector", S[tb_][:, 0:1024], S[tb_][:, 0:1024], S["xgateA"][:, 0:1024],
                   OP.mult, reads=tb2 + kb_cover, writes=tb2, tag=f"{tag}_tern")
                p.dma("sync", ternT[djj * 128:(djj + 1) * 128, qq * 1024:(qq + 1) * 1024],
                      S[tb_][:, 0:1024], "qSt", reads=tb2, writes=(ternT_nm,), tag=f"{tag}_stt")

    ternarize(wfc_n, wfc_t, HID, DM, "scale_fc", "rs_fc", rsfc_d, ternfcT,
              "fc_rsflat", "ternfcT", "fc")
    ternarize(wproj_n, wproj_t, DM, HID, "scale_pj", "rs_pj", rsproj_d, ternprojT,
              "pj_rsflat", "ternprojT", "pj")

    ALL_XG = tuple(f"xgA{c}" for c in range(NC_DI))
    ALL_XMA = tuple(f"xmA{c}" for c in range(NC_DI))
    p.dma("sync", S["xgateA"][0:NS, 0:HID], u_t, "qC", writes=ALL_XG, tag="us_ldu")
    p.dma("sync", S["xmainA"][0:NS, 0:HID],
          rsfc_d.unsqueeze(1).broadcast_to([1, NS, HID]),
          "qC", reads=("fc_rsflat",), writes=ALL_XMA, tag="us_ldrs")
    TT("vector", S["xgateA"][0:NS, 0:HID], S["xgateA"][0:NS, 0:HID],
       S["xmainA"][0:NS, 0:HID], OP.mult,
       reads=ALL_XG + ALL_XMA, writes=ALL_XG, tag="us_mul")
    p.dma("sync", us_d, S["xgateA"][0:NS, 0:HID], "qSt", reads=ALL_XG,
          writes=("us_d",), tag="us_st")

    STATR = S["statr"]
    mu_r = STATR[0:1, 0:T]; v1_r = STATR[0:1, T:2 * T]; w_r = STATR[0:1, 2 * T:3 * T]
    m2_r = w_r; sd_r = w_r; rstd_r = w_r; q_r = mu_r
    NH = NS // 2

    def ln_phase(xpre, xten, nch, scale_cols, bias_cols, scale_reads, outpre, outten, tag):
        for j in range(nch):
            sqb = "sq0" if j % 2 == 0 else "sq1"
            ACT(S[sqb][:], xten[:, j * T:(j + 1) * T], AF.Square,
                reads=(f"{xpre}{j}",), writes=(sqb,), tag=f"{tag}_sq{j}")
            MM(0, 1, S["ones_col"][:, 0:1], xten[:, j * T:(j + 1) * T],
               last_in_bank=(j == nch - 1), reads=(f"{xpre}{j}", "ones_col"),
               mslice=1, tag=f"{tag}_mmx{j}")
            MM(1, 1, S["ones_col"][:, 0:1], S[sqb][:],
               last_in_bank=(j == nch - 1), reads=(sqb, "ones_col"),
               mslice=1, tag=f"{tag}_mmq{j}")
        D = nch * 128
        TS("vector", mu_r, PS[0][0:1, T:2 * T], 1.0 / D, OP.mult,
           reads=("pb0",), writes=("mu_r",), tag=f"{tag}_mu")
        TS("vector", v1_r, PS[1][0:1, T:2 * T], 1.0 / D, OP.mult, EPS, OP.add,
           reads=("pb1",), writes=("v1_r",), tag=f"{tag}_v1")
        ACT(m2_r, mu_r, AF.Square, reads=("mu_r",), writes=("w_r",), tag=f"{tag}_m2")
        TT("vector", v1_r, v1_r, m2_r, OP.subtract,
           reads=("v1_r", "w_r"), writes=("v1_r",), tag=f"{tag}_var")
        ACT(sd_r, v1_r, AF.Sqrt, reads=("v1_r",), writes=("w_r",), tag=f"{tag}_sd")
        RECIP(rstd_r, sd_r, reads=("w_r",), writes=("w_r",), tag=f"{tag}_rstd")
        bcast_dma(S["rstd_rep"][:], rstd_r, "qLN", reads=("w_r",), writes=("rstd_rep",),
                  tag=f"{tag}_bc1")
        TT("vector", q_r, mu_r, rstd_r, OP.mult,
           reads=("mu_r", "w_r"), writes=("mu_r",), tag=f"{tag}_q")
        bcast_dma(S["q_rep"][:], q_r, "qLN", reads=("mu_r",), writes=("q_rep",), tag=f"{tag}_bc2")
        for j in range(nch):
            tm = "tmA" if j % 2 == 0 else "tmB"
            TT("vector", S[tm][:], xten[:, j * T:(j + 1) * T], S["rstd_rep"][:], OP.mult,
               reads=(f"{xpre}{j}", "rstd_rep"), writes=(tm,), tag=f"{tag}_s1{j}")
            TT("vector", S[tm][:], S[tm][:], S["q_rep"][:], OP.subtract,
               reads=(tm, "q_rep"), writes=(tm,), tag=f"{tag}_s2{j}")
            ACT(outten[:, j * T:(j + 1) * T], S[tm][:], AF.Identity,
                scale=scale_cols[:, j:j + 1], bias=bias_cols[:, j:j + 1],
                reads=(tm,) + scale_reads, writes=(f"{outpre}{j}",), tag=f"{tag}_ap{j}")

    def proj_phase(wdram, EE, rhs_ten, rhs_pre, nd, evac, tag, bufs=("wbA", "wbB"),
                   skip_djs=()):
        echunks = ceil_div(EE, 128)
        for g in range(ceil_div(echunks, 8)):
            e0 = g * 8
            ecnt = min(8, echunks - e0)
            ew = min(1024, EE - e0 * 128)
            for dj in range(nd):
                wb = bufs[dj % 2]
                for hh in range(2):
                    c0, c1 = hh * 512, min(ew, (hh + 1) * 512)
                    if c0 >= c1:
                        continue
                    if g == 0 and dj in skip_djs:
                        continue
                    p.dma("sync", S[wb][:, c0:c1],
                          wdram[dj * 128:(dj + 1) * 128, e0 * 128 + c0:e0 * 128 + c1],
                          f"{wb}{hh}", writes=(f"{wb}{hh}",),
                          tag=f"{tag}_ldw{g}_{dj}_{hh}")
                for j in range(ecnt):
                    m = min(128, EE - (e0 + j) * 128)
                    MM(j, 0, S[wb][:, j * 128:j * 128 + m],
                       rhs_ten[:, dj * T:(dj + 1) * T],
                       last_in_bank=(dj == nd - 1),
                       reads=(f"{wb}{j // 4}", f"{rhs_pre}{dj}"), mslice=m,
                       tag=f"{tag}_mm{g}_{dj}_{j}")
            for j in range(ecnt):
                evac(e0 + j, j)

    def xm(ti):
        return S["xmainA"] if ti % 2 == 0 else S["xmainB"]

    def xmp(ti):
        return "xmA" if ti % 2 == 0 else "xmB"

    def xg(ti):
        return S["xgateA"] if ti % 2 == 0 else S["xgateB"]

    def xgp(ti):
        return "xgA" if ti % 2 == 0 else "xgB"

    def A_ln(ti):
        t0 = ti * T
        xp = "x0_" if ti % 2 == 0 else "x1_"
        X = S["x0"] if ti % 2 == 0 else S["x1"]
        p.dma("sync", X[:].rearrange("p (j t) -> p j t", j=NC_DM),
              x_t.rearrange("(j p) l -> p j l", p=128)[:, :, t0:t0 + T],
              "qX", writes=tuple(f"{xp}{j}" for j in range(NC_DM)), tag=f"xin{ti}")
        ln_phase(xp, X, NC_DM, S["f1_c"], S["sa_c"], ("f1_c", "sa_c"),
                 "h", S["h"], f"t{ti}ln1")
        ln_phase("h", S["h"], NC_DM, S["lng_c"], S["lnb_c"], ("lng_c", "lnb_c"),
                 "xn", S["xn"], f"t{ti}ln2")
        for j in range(NC_DM):
            STT("vector", X[:, j * T:(j + 1) * T], S["h"][:, j * T:(j + 1) * T],
                S["ga_c"][:, j:j + 1], X[:, j * T:(j + 1) * T], OP.mult, OP.add,
                reads=(f"h{j}", "ga_c", f"{xp}{j}"), writes=(f"{xp}{j}",),
                tag=f"t{ti}xh{j}")

    def A_up1(ti):
        XM = xm(ti); xmn = xmp(ti)
        def evac_up1(ec, bank, ti=ti, XM=XM, xmn=xmn):
            CP("vector", S["Xmp"][:, KC - 1:KC - 1 + T], pslot(bank, 0),
               reads=(f"pb{bank}",), writes=("Xmp",), tag=f"t{ti}cv_ev{ec}")
            CP("vector", S["Xmp"][:, 0:KC - 1],
               S["convhalo"][:, ec * (KC - 1):(ec + 1) * (KC - 1)],
               reads=(f"convhalo{ec}",), writes=("Xmp",), tag=f"t{ti}cv_h{ec}")
            CP("vector", S["convhalo"][:, ec * (KC - 1):(ec + 1) * (KC - 1)],
               S["Xmp"][:, T:T + KC - 1],
               reads=("Xmp",), writes=(f"convhalo{ec}",), tag=f"t{ti}cv_sv{ec}")
            TS("vector", S["scA"][:], S["Xmp"][:, 0:T],
               S["convw_c"][:, ec * KC:ec * KC + 1], OP.mult,
               S["convb_c"][:, ec:ec + 1], OP.add,
               reads=("Xmp", "convw_c", "convb_c"), writes=("scA",), tag=f"t{ti}cv0_{ec}")
            for k in range(1, KC):
                STT("vector", S["scA"][:], S["Xmp"][:, k:k + T],
                    S["convw_c"][:, ec * KC + k:ec * KC + k + 1], S["scA"][:],
                    OP.mult, OP.add,
                    reads=("Xmp", "convw_c", "scA"), writes=("scA",), tag=f"t{ti}cv{k}_{ec}")
            ACT(XM[:, ec * T:(ec + 1) * T], S["scA"][:], AF.Silu,
                reads=("scA",), writes=(f"{xmn}{ec}",), tag=f"t{ti}cvs_{ec}")
        proj_phase(wup1_t, DI, S["xn"], "xn", NC_DM, evac_up1, f"t{ti}up1")

    def A_up2(ti):
        XG = xg(ti); xgn = xgp(ti)
        def evac_up2(ec, bank, ti=ti, XG=XG, xgn=xgn):
            ACT(XG[:, ec * T:(ec + 1) * T], pslot(bank, 0), AF.Silu,
                reads=(f"pb{bank}",), writes=(f"{xgn}{ec}",), tag=f"t{ti}xg_{ec}")
        proj_phase(wup2_t, DI, S["xn"], "xn", NC_DM, evac_up2, f"t{ti}up2")

    def BmCm_phase(ti):
        XM = xm(ti); xmn = xmp(ti)
        for dj in range(NC_DI):
            wb = "wbA" if dj % 2 == 0 else "wbB"
            p.dma("sync", S[wb][:, 0:32], wll_t[dj * 128:(dj + 1) * 128, 2048:2080],
                  f"{wb}0", writes=(f"{wb}0",), tag=f"t{ti}bc_ldw{dj}")
            MM(2, 1, S[wb][:, 0:16], XM[:, dj * T:(dj + 1) * T],
               last_in_bank=(dj == NC_DI - 1), reads=(f"{wb}0", f"{xmn}{dj}"),
               mslice=16, tag=f"t{ti}bmm{dj}")
            MM(3, 1, S[wb][:, 16:32], XM[:, dj * T:(dj + 1) * T],
               last_in_bank=(dj == NC_DI - 1), reads=(f"{wb}0", f"{xmn}{dj}"),
               mslice=16, tag=f"t{ti}cmm{dj}")
        ACT(S["Bm"][:], PS[2][0:NS, T:2 * T], AF.Identity, bias=S["bll_c"][0:NS, 16:17],
            reads=("pb2", "bll_c"), writes=("Bm",), tag=f"t{ti}bm_ev")
        ACT(S["Cm"][:], PS[3][0:NS, T:2 * T], AF.Identity, bias=S["bll_c"][0:NS, 17:18],
            reads=("pb3", "bll_c"), writes=("Cm",), tag=f"t{ti}cm_ev")

    def reps_phase(ti):
        for n in range(NS):
            bcast_dma(S["Bm_rep"][:, n * T:(n + 1) * T], S["Bm"][n:n + 1, :],
                      f"qR{n % 4}", reads=("Bm",), writes=(f"Bmr{n}",), tag=f"t{ti}bmr{n}")
        for n in range(NS):
            bcast_dma(S["C_rep"][:, n * T:(n + 1) * T], S["Cm"][n:n + 1, :],
                      f"qR{n % 4}", reads=("Cm",), writes=(f"Cr{n}",), tag=f"t{ti}cr{n}")

    def pp_wave(ti, cs, banks):
        XM = xm(ti); xmn = xmp(ti)
        e0 = cs[0]
        ew = 128 * len(cs)
        for dj in range(NC_DI):
            wb = "wbA" if dj % 2 == 0 else "wbB"
            for hh in range(2):
                c0, c1 = hh * 512, min(ew, (hh + 1) * 512)
                if c0 >= c1:
                    continue
                p.dma("sync", S[wb][:, c0:c1],
                      wll_t[dj * 128:(dj + 1) * 128, e0 * 128 + c0:e0 * 128 + c1],
                      f"{wb}{hh}", writes=(f"{wb}{hh}",),
                      tag=f"t{ti}pp_ldw{e0}_{dj}_{hh}")
            for jj, ec in enumerate(cs):
                MM(banks[jj], 0, S[wb][:, jj * 128:(jj + 1) * 128],
                   XM[:, dj * T:(dj + 1) * T],
                   last_in_bank=(dj == NC_DI - 1),
                   reads=(f"{wb}{jj // 4}", f"{xmn}{dj}"), tag=f"t{ti}ppmm{ec}_{dj}")

    def sp_wave(ti, cs, banks):
        for jj, ec in enumerate(cs):
            j = banks[jj]
            db = ("deltaA", "deltaB", "deltaC", "deltaD")[ec % 4]
            ACT(S["scB"][:], pslot(j, 0), AF.Relu, bias=S["bll_c"][:, ec:ec + 1],
                reads=(f"pb{j}", "bll_c"), writes=("scB",), tag=f"t{ti}sp_z{ec}")
            ACT(S["tmA"][:], pslot(j, 0), AF.Abs, bias=S["bll_c"][:, ec:ec + 1],
                reads=(f"pb{j}", "bll_c"), writes=("tmA",), tag=f"t{ti}sp_a{ec}")
            ACT(S["tmB"][:], S["tmA"][:], AF.Exp, scale=S["kconst"][:, 0:1],
                reads=("tmA", "kconst"), writes=("tmB",), tag=f"t{ti}sp_e{ec}")
            ACT(S["tmA"][:], S["tmB"][:], AF.Ln, bias=S["kconst"][:, 1:2],
                reads=("tmB", "kconst"), writes=("tmA",), tag=f"t{ti}sp_l{ec}")
            TT("gpsimd", S[db][:], S["scB"][:], S["tmA"][:], OP.add,
               reads=("scB", "tmA"), writes=(db,), tag=f"t{ti}sp_d{ec}")

    def da_half(ti, c, hh):
        db = ("deltaA", "deltaB", "deltaC", "deltaD")[c % 4]
        H = NS // 2
        gh = 2 * c + hh
        AB = S["AbarA"] if gh % 2 == 0 else S["AbarB"]
        abn = tuple(f"Ab{gh % 2}s{j}" for j in range(H))
        n0 = hh * H
        def fn_da(E, AB=AB, c=c, n0=n0, db=db):
            return E.tensor_tensor(
                AB[:].rearrange("p (n t) -> p n t", n=H),
                bass.AP(S[db], 0, [[T, 128], [0, H], [1, T]]),
                bass.AP(S["A_fm"], c * NS + n0, [[NC_DI * NS, 128], [1, H], [0, T]]),
                OP.mult)
        p.add("gpsimd", fn_da, reads=(db, "A_fm"), writes=abn, tag=f"t{ti}c{c}h{hh}da")

    def exp_half(ti, c, hh):
        H = NS // 2
        gh = 2 * c + hh
        AB = S["AbarA"] if gh % 2 == 0 else S["AbarB"]
        abn = tuple(f"Ab{gh % 2}s{j}" for j in range(H))
        ACT(AB[:], AB[:], AF.Exp, reads=abn, writes=abn, tag=f"t{ti}c{c}h{hh}ex")

    def cube_chunk(ti, c):
        XM = xm(ti); xmn = xmp(ti)
        H = NS // 2
        exp_half(ti, c, 0); exp_half(ti, c, 1)
        if c < NC_DI - 1:
            nxt = (ti, c + 1)
        elif ti + 1 < NT:
            nxt = (ti + 1, 0)
        else:
            nxt = None
        for hh in range(2):
            gh = 2 * c + hh
            AB = S["AbarA"] if gh % 2 == 0 else S["AbarB"]
            abn = tuple(f"Ab{gh % 2}s{j}" for j in range(H))
            gbn = tuple(f"G_s{j}" for j in range(H))
            n0 = hh * H
            for j in range(H):
                n = n0 + j
                STT("vector", S["Gbuf"][:, j * T:(j + 1) * T],
                    XM[:, c * T:(c + 1) * T],
                    S["iA_fm"][:, c * NS + n:c * NS + n + 1],
                    S["Bm_rep"][:, n * T:(n + 1) * T], OP.mult, OP.mult,
                    reads=(f"{xmn}{c}", "iA_fm", f"Bmr{n}"), writes=(f"G_s{j}",),
                    tag=f"t{ti}c{c}g{n}")
            STT("vector", S["Gbuf"][:], AB[:], -1.0, S["Gbuf"][:], OP.add, OP.mult,
                reads=abn + gbn, writes=gbn, tag=f"t{ti}c{c}h{hh}u")
            for j in range(H):
                n = n0 + j
                def fn_sc(E, j=j, n=n, c=c, AB=AB):
                    return E.tensor_tensor_scan(
                        S["Gbuf"][:, j * T:(j + 1) * T],
                        S["Gbuf"][:, j * T:(j + 1) * T],
                        AB[:, j * T:(j + 1) * T],
                        S["K_state"][:, c * NS + n:c * NS + n + 1],
                        OP.add, OP.mult)
                p.add("vector", fn_sc, reads=(f"G_s{j}", abn[j], f"Kst{c}"),
                      writes=(f"G_s{j}",), tag=f"t{ti}c{c}sc{n}")
            def fn_svk(E, out=S["K_state"][:, c * NS + n0:c * NS + n0 + H],
                       inp=bass.AP(S["Gbuf"], T - 1, [[H * T, 128], [T, H]])):
                return E.tensor_copy(out, inp)
            p.add("vector", fn_svk, reads=gbn, writes=(f"Kst{c}",), tag=f"t{ti}c{c}h{hh}ks")
            if nxt is not None:
                da_half(nxt[0], nxt[1], hh)
            TT("vector", S["Gbuf"][:], S["Gbuf"][:],
               S["C_rep"][:, n0 * T:(n0 + H) * T], OP.mult,
               reads=gbn + tuple(f"Cr{n0 + j}" for j in range(H)),
               writes=gbn, tag=f"t{ti}c{c}h{hh}ch")
            red_out = S["yK"][:] if hh == 0 else S["t1b"][:]
            red_nm = "yK" if hh == 0 else "t1b"
            def fn_red(E, red_out=red_out):
                return E.tensor_reduce(
                    red_out,
                    bass.AP(S["Gbuf"], 0, [[H * T, 128], [1, T], [T, H]]),
                    AX.X, OP.add)
            p.add("vector", fn_red, reads=gbn, writes=(red_nm,), tag=f"t{ti}c{c}h{hh}red")
        TT("vector", S["yK"][:], S["yK"][:], S["t1b"][:], OP.add,
           reads=("yK", "t1b"), writes=("yK",), tag=f"t{ti}c{c}ry")
        TT("vector", xg(ti)[:, c * T:(c + 1) * T], S["yK"][:],
           xg(ti)[:, c * T:(c + 1) * T], OP.mult,
           reads=("yK", f"{xgp(ti)}{c}"), writes=(f"{xgp(ti)}{c}",), tag=f"t{ti}c{c}gt")

    def C1a_phase(ti):
        xp = "x0_" if ti % 2 == 0 else "x1_"
        X = S["x0"] if ti % 2 == 0 else S["x1"]
        XM = xm(ti)

        def evac_down(ec, bank, ti=ti, xp=xp, X=X):
            ACT(S["scB"][:], pslot(bank, 0), AF.Identity, bias=S["bdown_c"][:, ec:ec + 1],
                reads=(f"pb{bank}", "bdown_c"), writes=("scB",), tag=f"t{ti}dn_ev{ec}")
            STT("vector", X[:, ec * T:(ec + 1) * T], S["scB"][:],
                S["ga_c"][:, ec:ec + 1], X[:, ec * T:(ec + 1) * T], OP.mult, OP.add,
                reads=("scB", "ga_c", f"{xp}{ec}"), writes=(f"{xp}{ec}",), tag=f"t{ti}dn2{ec}")

        proj_phase(wdown_t, DM, xg(ti), xgp(ti), NC_DI, evac_down, f"t{ti}dn",
                   bufs=("wbC", "wbD"), skip_djs=(0, 1))

        ln_phase(xp, X, NC_DM, S["f2_c"], S["sb_c"], ("f2_c", "sb_c"),
                 "h", S["h"], f"t{ti}ln3")

        for dj in range(NC_DM):
            MM(2, 0, S["v_c"][:, dj * NS:(dj + 1) * NS], S["h"][:, dj * T:(dj + 1) * T],
               last_in_bank=(dj == NC_DM - 1), reads=(f"v_c{dj}", f"h{dj}"),
               mslice=NS, tag=f"t{ti}a1_{dj}")
        ACT(S["scA"][0:NS, :], PS[2][0:NS, 0:T], AF.Identity, reads=("pb2",),
            writes=("scA",), tag=f"t{ti}a1ev")
        MM(3, 0, S["fbgT"][:], S["scA"][0:NS, 0:T], last_in_bank=True,
           mslice=NS, reads=("fbgT", "scA"), tag=f"t{ti}a2")
        ACT(S["Xmp"][0:NS, 0:T], PS[3][0:NS, 0:T], AF.Identity, reads=("pb3",),
            writes=("Xmp",), tag=f"t{ti}a2ev")

    def uw_load(ti, k):
        ub = k % 4
        p.dma("sync", S["ubuf"][:, ub * 128:(ub + 1) * 128],
              us_d[:, k * 128:(k + 1) * 128], f"ubuf{ub % 2}",
              reads=("us_d",), writes=(f"ubuf{ub}",), tag=f"t{ti}uw{k}")

    def ffn_front(ti, k):
        fb = 0 if k % 2 == 0 else 1
        wfb = "wbA" if k % 2 == 0 else "wbB"
        wpb = "wbC" if k % 2 == 0 else "wbD"
        ub = k % 4
        for hh in range(2):
            p.dma("sync",
                  S[wfb][:].rearrange("p (j e) -> p j e", j=NC_DM)[:, hh * 4:hh * 4 + 4],
                  ternfcT.rearrange("(j p) e -> p j e", p=128)
                  [:, hh * 4:hh * 4 + 4, k * 128:(k + 1) * 128],
                  f"{wfb}{hh}", reads=("ternfcT",), writes=(f"{wfb}{hh}",),
                  tag=f"t{ti}fcw{k}_{hh}")
            p.dma("sync", S[wpb][:, hh * 512:(hh + 1) * 512],
                  ternprojT[k * 128:(k + 1) * 128, hh * 512:(hh + 1) * 512],
                  f"{wpb}{hh}", reads=("ternprojT",), writes=(f"{wpb}{hh}",),
                  tag=f"t{ti}pjw{k}_{hh}")
        if k + 2 < NC_HID:
            uw_load(ti, k + 2)
        MM(fb, 1, S["ubuf"][:, ub * 128:(ub + 1) * 128], S["Xmp"][0:NS, 0:T],
           last_in_bank=False, reads=(f"ubuf{ub}", "Xmp"), tag=f"t{ti}L{k}")
        for dj in range(NC_DM):
            MM(fb, 1, S[wfb][:, dj * 128:(dj + 1) * 128], S["h"][:, dj * T:(dj + 1) * T],
               last_in_bank=(dj == NC_DM - 1), reads=(f"{wfb}{dj // 4}", f"h{dj}"),
               tag=f"t{ti}F{k}_{dj}")

    def ffn_epi(ti, k):
        fb = 0 if k % 2 == 0 else 1
        pp_ = k % 2
        ACT(S[f"fA{pp_}"][:], pslot(fb, 1), AF.Relu, scale=S["scale_fc"][:, k:k + 1],
            reads=(f"pb{fb}", f"scale_fc{k}"), writes=(f"fA{pp_}",), tag=f"t{ti}tr{k}")
        ACT(S[f"fB{pp_}"][:], S[f"fA{pp_}"][:], AF.Square,
            reads=(f"fA{pp_}",), writes=(f"fB{pp_}",), tag=f"t{ti}r2{k}")

    def ffn_fc2(ti, k):
        pp_ = k % 2
        for j in range(NC_DM):
            bank = 4 + j // 2
            col = j % 2
            MM(bank, col, S["wbC" if k % 2 == 0 else "wbD"][:, j * 128:(j + 1) * 128],
               S[f"fB{pp_}"][:],
               last_in_bank=(k == NC_HID - 1 and j % 2 == 1),
               reads=(("wbC" if k % 2 == 0 else "wbD") + str(j // 4), f"fB{pp_}"),
               tag=f"t{ti}O{k}_{j}")

    def ffn_ks(ti, ks):
        for k in ks:
            if k == 0:
                uw_load(ti, 0); uw_load(ti, 1)
            ffn_front(ti, k)
            ffn_epi(ti, k)
            if k > 0:
                ffn_fc2(ti, k - 1)
            if k == NC_HID - 1:
                ffn_fc2(ti, k)

    def C2_phase(ti):
        t0 = ti * T
        xp = "x0_" if ti % 2 == 0 else "x1_"
        X = S["x0"] if ti % 2 == 0 else S["x1"]
        for j in range(NC_DM):
            bank = 4 + j // 2
            col = j % 2
            ACT(S["scA"][:], pslot(bank, col), AF.Identity, scale=S["scale_pj"][:, j:j + 1],
                reads=(f"pb{bank}", f"scale_pj{j}"), writes=("scA",), tag=f"t{ti}fo_e{j}")
            STT("vector", X[:, j * T:(j + 1) * T], S["scA"][:], S["gb_c"][:, j:j + 1],
                X[:, j * T:(j + 1) * T], OP.mult, OP.add,
                reads=("scA", "gb_c", f"{xp}{j}"), writes=(f"{xp}{j}",), tag=f"t{ti}fo_x{j}")
        p.dma("scalar", out_t.rearrange("(j p) l -> p j l", p=128)[:, :, t0:t0 + T],
              X[:].rearrange("p (j t) -> p j t", j=NC_DM),
              "qOut", reads=tuple(f"{xp}{j}" for j in range(NC_DM)), tag=f"xout{ti}")

    def dn_prefetch(ti):
        for dj in (0, 1):
            wb = ("wbC", "wbD")[dj % 2]
            for hh in range(2):
                p.dma("sync", S[wb][:, hh * 512:(hh + 1) * 512],
                      wdown_t[dj * 128:(dj + 1) * 128, hh * 512:(hh + 1) * 512],
                      f"{wb}{hh}", writes=(f"{wb}{hh}",), tag=f"t{ti}dn_pre{dj}_{hh}")

    def sp1(ti, c, b):
        sp_wave(ti, [c], [b])

    A_ln(0); A_up1(0); A_up2(0)
    BmCm_phase(0)
    reps_phase(0)
    pp_wave(0, list(range(0, 8)), list(range(8)))
    sp1(0, 0, 0); sp1(0, 1, 1); sp1(0, 2, 2); sp1(0, 3, 3)
    da_half(0, 0, 0); da_half(0, 0, 1)
    for ti in range(NT):
        ffn = (lambda ks: ffn_ks(ti - 1, ks)) if ti > 0 else (lambda ks: None)
        cube_chunk(ti, 0); sp1(ti, 4, 4)
        cube_chunk(ti, 1); sp1(ti, 5, 5)
        if ti > 0:
            C1a_phase(ti - 1)
        cube_chunk(ti, 2); sp1(ti, 6, 6)
        pp_wave(ti, [8, 9], [2, 3])
        cube_chunk(ti, 3); sp1(ti, 7, 7)
        cube_chunk(ti, 4); sp1(ti, 8, 2); ffn(range(0, 4))
        cube_chunk(ti, 5); sp1(ti, 9, 3); ffn(range(4, 8))
        pp_wave(ti, [10, 11], [2, 3])
        cube_chunk(ti, 6); sp1(ti, 10, 2); ffn(range(8, 12))
        cube_chunk(ti, 7); sp1(ti, 11, 3); ffn(range(12, 16))
        pp_wave(ti, [12, 13, 14, 15], [0, 1, 2, 3])
        cube_chunk(ti, 8); ffn(range(16, 20))
        cube_chunk(ti, 9); ffn(range(20, 24))
        sp1(ti, 12, 0); sp1(ti, 13, 1)
        cube_chunk(ti, 10); ffn(range(24, 28))
        cube_chunk(ti, 11); ffn(range(28, 32))
        if ti > 0:
            C2_phase(ti - 1)
        sp1(ti, 14, 2); sp1(ti, 15, 3)
        if ti + 1 < NT:
            A_ln(ti + 1)
            A_up1(ti + 1)
        cube_chunk(ti, 12)
        cube_chunk(ti, 13)
        if ti + 1 < NT:
            A_up2(ti + 1)
            BmCm_phase(ti + 1)
            pp_wave(ti + 1, list(range(0, 8)), list(range(8)))
            sp1(ti + 1, 0, 0); sp1(ti + 1, 1, 1)
        cube_chunk(ti, 14)
        if ti + 1 < NT:
            sp1(ti + 1, 2, 2)
        if ti + 1 < NT:
            dn_prefetch(ti)
        cube_chunk(ti, 15)
        if ti + 1 < NT:
            reps_phase(ti + 1)
            sp1(ti + 1, 3, 3)
    dn_prefetch(NT - 1)
    C1a_phase(NT - 1)
    ffn_ks(NT - 1, range(NC_HID))
    C2_phase(NT - 1)

    _LAST_PLAN[0] = p
    p.finalize()
    with contextlib.ExitStack() as st:
        sems = {}
        for name in sorted(p.final_counts):
            sems[name] = st.enter_context(nc.semaphore(name))
        block = st.enter_context(nc.Block())
        p.emit(block, sems)

        fc = dict(p.final_counts)
        def fin_scalar(E, fc=fc):
            if fc.get("d_qOut"):
                E.wait_ge(sems["d_qOut"], fc["d_qOut"])
        def fin_sync(E, fc=fc):
            for k in sorted(fc):
                if k.startswith("d_") and k != "d_qOut" and fc[k]:
                    E.wait_ge(sems[k], fc[k])
        block.sync(fin_sync)
        block.scalar(fin_scalar)
    return nc


def _prep_core_inputs(b, I):
    c = np.ascontiguousarray
    fm = lambda v, nch: c(np.asarray(v, np.float32).reshape(nch, 128).T)
    bll_v = np.asarray(I["L0_attn_b_ll"], np.float32)
    bll_p = np.zeros(18 * 128, np.float32)
    bll_p[:2048] = bll_v[:2048]
    bll_p[16 * 128:16 * 128 + 16] = bll_v[2048:2064]
    bll_p[17 * 128:17 * 128 + 16] = bll_v[2064:2080]
    convw = np.asarray(I["L0_attn_conv_w"], np.float32)[:, 0, :]
    alog = np.asarray(I["L0_attn_A_log"], np.float32)
    m = {
        "x_t": c(np.asarray(I["L0_attn_X"][b], np.float32).T),
        "wup1_t": c(np.asarray(I["L0_attn_W_up1"], np.float32).T),
        "wup2_t": c(np.asarray(I["L0_attn_W_up2"], np.float32).T),
        "wll_t": c(np.asarray(I["L0_attn_W_ll"], np.float32).T),
        "wdown_t": c(np.asarray(I["L0_attn_W_down"], np.float32).T),
        "wfc_t": c(np.asarray(I["L0_ffn_w_fc"], np.float32).T),
        "wfc_n": c(np.asarray(I["L0_ffn_w_fc"], np.float32)),
        "wproj_t": c(np.asarray(I["L0_ffn_w_proj"], np.float32).T),
        "wproj_n": c(np.asarray(I["L0_ffn_w_proj"], np.float32)),
        "u_t": c(np.asarray(I["L0_ffn_u"], np.float32).T),
        "v_t": c(np.asarray(I["L0_ffn_v"], np.float32).T),
        "fastb_t": c(np.asarray(I["L0_ffn_fast_b"], np.float32).T),
        "wmod": c(np.asarray(I["L0_wmod"], np.float32)),
        "cfm": fm(I["L0_c"][b], 8),
        "lng": fm(I["L0_attn_ln_g"], 8),
        "lnb": fm(I["L0_attn_ln_b"], 8),
        "bll": fm(bll_p, 18),
        "bdown": fm(I["L0_attn_b_down"], 8),
        "convw": c(convw.reshape(16, 128, 4).transpose(1, 0, 2).reshape(128, 64)),
        "convb": fm(I["L0_attn_conv_b"], 16),
        "alog_fm": c(alog.reshape(16, 128, 16).transpose(1, 0, 2).reshape(128, 256)),
        "gate_in": np.asarray(I["L0_ffn_gate"], np.float32).reshape(1, 1),
        "ident_in": np.eye(128, dtype=np.float32),
    }
    return {k: np.ascontiguousarray(v, dtype=np.float32) for k, v in m.items()}


def kernel(**inputs):
    nc = build_program(NT=L_FULL // T)
    in_maps = [_prep_core_inputs(b, inputs) for b in range(NCHIP)]
    res = run_bass_kernel_spmd(nc, in_maps, core_ids=list(range(NCHIP)))
    out = np.stack([np.asarray(r["out_t"]).T for r in res.results], axis=0)
    return np.ascontiguousarray(out, dtype=np.float32)
